# version 11
# baseline (speedup 1.0000x reference)
import contextlib
import numpy as np
import concourse.bass as bass
import concourse.mybir as mybir

F32 = mybir.dt.float32
BF16 = mybir.dt.bfloat16
FP16 = mybir.dt.float16
FP8 = mybir.dt.float8e4
AF = mybir.ActivationFunctionType
AX = mybir.AxisListType
ALU = mybir.AluOpType

NEG_BIG = -3.0e38
SCALE = 1.0 / float(np.sqrt(128.0))
LAYER_DEPTH = 6 / 12


def make_cfg(D=2048, S=4096, n_cores=8):
    c = {}
    c["D"] = D
    c["S"] = S
    c["R"] = S // 2
    c["F"] = 3 * D
    c["nD"] = D // 128
    c["nT"] = S // 128
    c["nR"] = c["R"] // 128
    c["nRB"] = c["R"] // 512
    c["nG"] = c["R"] // 512
    c["nFB"] = D // 512
    c["nFS"] = c["F"] // 128
    c["n_cores"] = n_cores
    c["groups"] = [[i, i + 1] for i in range(0, n_cores, 2)]
    c["all_cores"] = [list(range(n_cores))]
    return c


class Prog:
    ENGS = ("tensor", "vector", "scalar", "gpsimd", "sync")

    def __init__(self, nc, es):
        self.nc = nc
        self.es = es
        self.stream = {e: [] for e in self.ENGS}
        self.sems = {}
        self.cnt = {}
        self.wm = {e: {} for e in self.ENGS}

    def sem(self, *names):
        for n in names:
            self.sems[n] = self.es.enter_context(self.nc.semaphore(n))
            self.cnt[n] = 0

    def now(self, name):
        return (name, self.cnt.get(name, 0))

    def op(self, eng, fn, waits=(), inc=None, n=1):
        ws = []
        for w in waits:
            if w is None:
                continue
            s, v = w
            if v > self.wm[eng].get(s, 0):
                ws.append((s, v))
                self.wm[eng][s] = v
        tok = None
        if inc is not None:
            self.cnt[inc] += n
            tok = (inc, self.cnt[inc])

        def run(e, fn=fn, ws=tuple(ws), inc=inc, n=n):
            for s, v in ws:
                e.wait_ge(self.sems[s], v)
            r = fn(e)
            if inc is not None:
                r.then_inc(self.sems[inc], n)
        self.stream[eng].append(run)
        return tok

    def dma(self, eng, out, in_, waits=(), chan="misc"):
        name = "d_" + chan
        if name not in self.sems:
            self.sem(name)
        return self.op(eng, lambda e, o=out, i=in_: e.dma_start(out=o, in_=i),
                       waits=waits, inc=name, n=16)

    def cc(self, fn, waits=(), chan="cc0"):
        if chan not in self.sems:
            self.sem(chan)
        return self.op("gpsimd", fn, waits=waits, inc=chan, n=1)

    def wait_only(self, eng, toks):
        ws = []
        for w in toks:
            if w is None:
                continue
            s, v = w
            if v > self.wm[eng].get(s, 0):
                ws.append((s, v))
                self.wm[eng][s] = v
        if ws:
            def run(e, ws=tuple(ws)):
                for s, v in ws:
                    e.wait_ge(self.sems[s], v)
            self.stream[eng].append(run)

    def finalize(self):
        block = self.es.enter_context(self.nc.Block())
        st = self.stream

        @block.tensor
        def _(e):
            for f in st["tensor"]:
                f(e)

        @block.vector
        def _(e):
            for f in st["vector"]:
                f(e)

        @block.scalar
        def _(e):
            for f in st["scalar"]:
                f(e)

        @block.gpsimd
        def _(e):
            for f in st["gpsimd"]:
                f(e)

        @block.sync
        def _(e):
            for f in st["sync"]:
                f(e)


def build(cfg):
    D, S, R, F = cfg["D"], cfg["S"], cfg["R"], cfg["F"]
    nD, nT, nR, nRB, nG, nFB, nFS = (cfg[k] for k in
                                     ("nD", "nT", "nR", "nRB", "nG", "nFB", "nFS"))
    nQC = 4

    nc = bass.Bass("TRN2", target_bir_lowering=False)
    es = contextlib.ExitStack()
    p = Prog(nc, es)

    dt = nc.dram_tensor
    x_row = dt("x_row", [R, D], F32, kind="ExternalInput").ap()
    xT = dt("xT", [D, R], F32, kind="ExternalInput").ap()
    attn_w = dt("attn_w", [D, F], F32, kind="ExternalInput").ap()
    attn_b_t = dt("attn_b_t", [128, nFS], F32, kind="ExternalInput").ap()
    b_v_row = dt("b_v_row", [1, D], F32, kind="ExternalInput").ap()
    proj_w = dt("proj_w", [D, D], F32, kind="ExternalInput").ap()
    b_p_row = dt("b_p_row", [1, D], F32, kind="ExternalInput").ap()
    gate_row = dt("gate_row", [1, 128], F32, kind="ExternalInput").ap()
    w01 = dt("w01", [1, 2], F32, kind="ExternalInput").ap()
    ffn_wT = dt("ffn_wT", [D, D], BF16, kind="ExternalInput").ap()
    ffn_b_row = dt("ffn_b_row", [1, D], BF16, kind="ExternalInput").ap()
    id_f32_in = dt("id_f32", [128, 128], F32, kind="ExternalInput").ap()
    id_bf_in = dt("id_bf", [128, 128], BF16, kind="ExternalInput").ap()
    out_hbm = dt("out", [R, D], F32, kind="ExternalOutput").ap()

    qT_hbm = dt("qT_hbm", [nG * 128, nD * 512], BF16).ap()
    kT_own = dt("kT_own", [(nT // 2) * 128, nD * 128], BF16)
    kT_ag = dt("kT_ag", [nT * 128, nD * 128], BF16)
    vR_own = dt("vR_own", [R, D], BF16)
    vR_ag = dt("vR_ag", [S, D], BF16)
    kcam_own = dt("kcam_own", [128, R], BF16)
    vcam_own = dt("vcam_own", [R, 128], BF16)
    kcam_ag = dt("kcam_ag", [256, R], BF16)
    vcam_ag = dt("vcam_ag", [S, 128], BF16)
    xln1_hbm = dt("xln1_hbm", [R, D], F32).ap()
    ctxT_hbm = dt("ctxT_hbm", [nG * 128, nD * 512], BF16).ap()
    xs_in = dt("xs_in", [1, 16], F32)
    xs_out = dt("xs_out", [1, 16], F32)

    A_COLS = max(nD * R, nT * 512 + nD * 512, nD * D)
    WBF_COLS = 2 * nD * 512
    NBK = min(6, nD)
    KTB_SZ = 2 * (2 * nD * 128)
    VTB_SZ = 2 * (4 * NBK * 128)
    CSTG_SZ = nD * 512
    if VTB_SZ + CSTG_SZ <= WBF_COLS:
        vtb_of = 0
        cstg_of = VTB_SZ
        kcam_of = WBF_COLS
        vcam_of = kcam_of + nT * 128
        ktb_of = vcam_of + nT * 128
        k_prefetch = True
        B_COLS = max(ktb_of + KTB_SZ, nD * D, nD * R)
    else:
        kcam_of = WBF_COLS
        vcam_of = kcam_of + nT * 128
        ktb_of = vcam_of + nT * 128
        vtb_of = ktb_of + KTB_SZ
        cstg_of = vtb_of + VTB_SZ
        k_prefetch = False
        B_COLS = max(cstg_of + CSTG_SZ, nD * D, nD * R)

    T = lambda name, shp, d: es.enter_context(nc.sbuf_tensor(name, shp, d))
    A = T("A", [128, A_COLS], BF16)
    Bb = T("Bb", [128, B_COLS], BF16)
    C = T("C", [128, 4096], F32)
    S16 = T("S16", [128, nT * 512], FP16)
    Mb = T("Mb", [128, D], F32)
    M2 = T("M2", [128, 1024], F32)
    MS = T("MS", [128, 128], F32)
    IDf = T("IDf", [128, 128], F32)
    IDb = T("IDb", [128, 128], BF16)
    G = T("G", [128, 128], F32)
    ATB = T("ATB", [128, nFS], F32)
    ROWA = T("ROWA", [1, 512], F32)
    ROWB = T("ROWB", [1, 2048], BF16)
    ONEROW = T("ONEROW", [1, 128], F32)
    ONECB = T("ONECB", [128, 1], BF16)
    ONEROWB = T("ONEROWB", [1, 128], BF16)
    STG = T("STG", [128, 4 * 512], BF16)
    TST = T("TST", [128, 512], BF16)
    TSQ = T("TSQ", [128, 128], F32)
    TSQ2 = T("TSQ2", [128, 128], F32)
    TSQB = T("TSQB", [128, 128], BF16)

    MbB = Mb[:].bitcast(BF16)
    M2B = M2[:].bitcast(BF16)
    S8 = S16[:].bitcast(FP8)
    S16B = S16[:].bitcast(BF16)
    SF = S16[:].bitcast(F32)

    MS_rec = 4
    MS_cmax = 8
    MS_bn = 16
    MS_ag = 40
    MS_nm = 42
    MS_rs = 43
    MS_eps = 44
    MS_gmax = 45
    MS_lw = 46
    MS_lwS = 50
    MS_xsr = 47
    MS_xsws = 48
    MS_wsr = 49
    MS_tmp = 51
    MS_tmp2 = 52
    MS_wmax = 53
    MS_w01 = 90

    PS = [es.enter_context(nc.psum_tensor("ps%d" % i, [128, 512], F32))
          for i in range(8)]

    p.sem("T", "V", "Sc", "Gp")

    ps_busy = [("Sc", 0)] * 8

    def mm(psv, lhsT, rhs, start, stop, waits=(), inc=None):
        return p.op("tensor",
                    lambda e, o=psv, l=lhsT, r=rhs, st=start, sp=stop:
                    e.matmul(o, l, r, start=st, stop=sp),
                    waits=waits, inc=inc)

    def tr(psv, in_, ident, waits=(), inc="T"):
        return p.op("tensor",
                    lambda e, o=psv, i=in_, d=ident: e.transpose(o, i, d),
                    waits=waits, inc=inc)

    def act(out, in_, func=AF.Copy, bias=None, scale=None, waits=(), inc="Sc"):
        if bias is not None and not isinstance(bias, float) and func == AF.Copy:
            func = AF.Identity
        def f(e, out=out, in_=in_, func=func, bias=bias, scale=scale):
            kw = {}
            if bias is not None:
                kw["bias"] = bias
            if scale is not None:
                kw["scale"] = scale
            return e.activation(out, in_, func, **kw)
        return p.op("scalar", f, waits=waits, inc=inc)

    def vop(fn, waits=(), inc="V"):
        return p.op("vector", fn, waits=waits, inc=inc)

    def gop(fn, waits=(), inc="Gp"):
        return p.op("gpsimd", fn, waits=waits, inc=inc)

    CS = [C[:, 1024 * i:1024 * i + 1024] for i in range(4)]
    cs_busy = [None] * 4
    cs_rr = [0]

    def c_slot(allowed=(0, 1, 2, 3)):
        for _ in range(4):
            i = cs_rr[0] % 4
            cs_rr[0] += 1
            if i in allowed:
                return i
        return allowed[0]

    p.dma("sync", IDf[:], id_f32_in, chan="ini")
    p.dma("sync", IDb[:], id_bf_in, chan="ini")
    p.dma("sync", ATB[:], attn_b_t, chan="ini")
    p.dma("sync", ROWA[0:1, 0:128], gate_row, chan="ini")
    t_ld0 = p.dma("sync", MS[0:1, MS_w01:MS_w01 + 2], w01, chan="ini")
    vop(lambda e: e.memset(ONEROW[:], 1.0))
    vop(lambda e: e.memset(ONECB[:], 1.0))
    vop(lambda e: e.memset(ONEROWB[:], 1.0))
    vop(lambda e: e.memset(MS[0:1, 96:112], 0.0))
    t_ones = vop(lambda e: e.memset(MS[:, MS_eps:MS_eps + 1], 1e-5))
    t = mm(PS[0][:, 0:128], ONEROW[0:1, :], ROWA[0:1, 0:128], True, True,
           waits=[t_ld0, t_ones], inc="T")
    t_G = act(G[:], PS[0][:, 0:128], waits=[t])
    ps_busy[0] = t_G
    t = vop(lambda e: e.tensor_scalar(MS[0:1, MS_tmp:MS_tmp + 1],
                                      MS[0:1, MS_w01 + 1:MS_w01 + 2], LAYER_DEPTH,
                                      None, op0=ALU.mult), waits=[t_ld0])
    t = vop(lambda e: e.tensor_tensor(MS[0:1, MS_tmp:MS_tmp + 1],
                                      MS[0:1, MS_tmp:MS_tmp + 1],
                                      MS[0:1, MS_w01:MS_w01 + 1], op=ALU.add),
            waits=[t])
    t = act(MS[0:1, MS_tmp:MS_tmp + 1], MS[0:1, MS_tmp:MS_tmp + 1], AF.Sigmoid,
            waits=[t])
    t = mm(PS[0][:, 128:129], ONEROW[0:1, :], MS[0:1, MS_tmp:MS_tmp + 1], True,
           True, waits=[t, ps_busy[0]], inc="T")
    t_lw = act(MS[:, MS_lw:MS_lw + 1], PS[0][:, 128:129], waits=[t])
    ps_busy[0] = t_lw
    t_lw = vop(lambda e: e.tensor_scalar(MS[:, MS_lwS:MS_lwS + 1],
                                         MS[:, MS_lw:MS_lw + 1], SCALE, None,
                                         op0=ALU.mult), waits=[t_lw])

    t_prev = None
    for s in range(D // 512):
        td = p.dma("sync", ROWA[0:1, :], b_v_row[0:1, 512 * s:512 * s + 512],
                   waits=[t_G if s == 0 else t_prev], chan="row")
        t_prev = vop(lambda e, s=s: e.tensor_copy(ROWB[0:1, 512 * s:512 * s + 512],
                                                  ROWA[0:1, :]), waits=[td])
    t_bv = t_prev

    nxh = (R + 1023) // 1024
    t_xc = [None] * nD
    for c in range(nD):
        for h in range(nxh):
            w = min(1024, R - 1024 * h)
            i = c_slot()
            td = p.dma("sync", CS[i][:, 0:w],
                       xT[128 * c:128 * c + 128, 1024 * h:1024 * h + w],
                       waits=[cs_busy[i]], chan="cs%d" % i)
            tc = vop(lambda e, c=c, h=h, w=w, i=i:
                     e.tensor_copy(A[:, R * c + 1024 * h:R * c + 1024 * h + w],
                                   CS[i][:, 0:w]), waits=[td])
            cs_busy[i] = tc
        t_xc[c] = tc

    sched = [("K", 0), ("V", 0)]
    sched += [("K", i) for i in range(1, nFB)]
    sched += [("V", i) for i in range(1, nFB)]
    sched += [("Q", i) for i in range(nFB)]
    sec_base = {"Q": 0, "K": D, "V": 2 * D}

    WBF = [Bb[:, 0:nD * 512], Bb[:, nD * 512:2 * nD * 512]]
    wbf_busy = [None, None]
    stg_busy = [None] * 4
    store_last = {"K": None, "V": None, "Q": None}
    cam_store_last = None

    def stg_now():
        return [p.now("d_stg%d" % i) for i in range(4) if "d_stg%d" % i in p.sems]

    WSF = [SF[:, 1024 * i:1024 * i + 1024] for i in range(min(8, (2 * S) // 1024))]
    nws = len(WSF)
    wsf_busy = [None] * nws
    wsf_rr = [0]
    wq = 0
    for fbi, (sec, fb) in enumerate(sched):
        fglob = sec_base[sec] + 512 * fb
        wslot = fbi % 2
        t_wc = [None] * nD
        for pc in range(nD // 2):
            d0 = pc * 256
            i = wsf_rr[0] % nws
            wsf_rr[0] += 1
            td = p.dma("sync", WSF[i][:, 0:1024].rearrange("p (a f) -> p a f", f=512),
                       attn_w[d0:d0 + 256, fglob:fglob + 512]
                       .rearrange("(a p) f -> p a f", p=128),
                       waits=[wsf_busy[i]], chan="ws%d" % i)
            tc = act(WBF[wslot][:, 1024 * pc:1024 * pc + 1024], WSF[i][:, 0:1024],
                     waits=[td, wbf_busy[wslot]])
            wsf_busy[i] = tc
            t_wc[2 * pc] = t_wc[2 * pc + 1] = tc

        last_T = None
        if sec in ("K", "Q"):
            for rb in range(nRB):
                for fs in range(4):
                    bank = (rb * 4 + fs) % 6
                    tmm = None
                    for c in range(nD):
                        tmm = mm(PS[bank][:, 0:512],
                                 WBF[wslot][:, 512 * c + 128 * fs:512 * c + 128 * fs + 128],
                                 A[:, R * c + 512 * rb:R * c + 512 * rb + 512],
                                 start=(c == 0), stop=(c == nD - 1),
                                 waits=[t_wc[c], t_xc[c]] +
                                       ([ps_busy[bank]] if c == 0 else []),
                                 inc=("T" if c == nD - 1 else None))
                    last_T = tmm
                    sl = wq % 4
                    wq += 1
                    bias_col = (fglob + 128 * fs) // 128
                    te = act(STG[:, 512 * sl:512 * sl + 512], PS[bank][:, 0:512],
                             bias=ATB[:, bias_col:bias_col + 1],
                             waits=[tmm, stg_busy[sl]])
                    ps_busy[bank] = te
                    if sec == "Q":
                        cg = fb * 4 + fs
                        tdo = p.dma("gpsimd",
                                    qT_hbm[128 * rb:128 * rb + 128,
                                           512 * cg:512 * cg + 512],
                                    STG[:, 512 * sl:512 * sl + 512],
                                    waits=[te], chan="stg%d" % sl)
                        store_last["Q"] = tdo
                    else:
                        ktile0 = rb * 4
                        cg = fb * 4 + fs
                        outap = kT_own.ap() \
                            .rearrange("(t p) q -> t p q", p=128)[ktile0:ktile0 + 4,
                                                                  :, 128 * cg:128 * cg + 128] \
                            .rearrange("t p k -> p t k")
                        inap = STG[:, 512 * sl:512 * sl + 512] \
                            .rearrange("p (t k) -> p t k", k=128)
                        tdo = p.dma("gpsimd", outap, inap, waits=[te], chan="stg%d" % sl)
                        store_last["K"] = tdo
                        if fb == 0 and fs == 0:
                            tdo = p.dma("gpsimd", kcam_own.ap()[:, 512 * rb:512 * rb + 512],
                                        STG[:, 512 * sl:512 * sl + 512],
                                        waits=[te], chan="stg%d" % sl)
                            cam_store_last = tdo
                    stg_busy[sl] = tdo
        else:
            for rc in range(nR):
                bank = rc % 6
                for c in range(nD):
                    mm(PS[bank][:, 0:512],
                       A[:, R * c + 128 * rc:R * c + 128 * rc + 128],
                       WBF[wslot][:, 512 * c:512 * c + 512],
                       start=(c == 0), stop=False,
                       waits=[t_wc[c], t_xc[c]] +
                             ([ps_busy[bank], t_bv] if c == 0 else []))
                tmm = mm(PS[bank][:, 0:512], ONEROWB[0:1, :],
                         ROWB[0:1, 512 * fb:512 * fb + 512],
                         start=False, stop=True, inc="T")
                last_T = tmm
                sl = wq % 4
                wq += 1
                te = act(STG[:, 512 * sl:512 * sl + 512], PS[bank][:, 0:512],
                         waits=[tmm, stg_busy[sl]])
                ps_busy[bank] = te
                tdo = p.dma("gpsimd",
                            vR_own.ap()[128 * rc:128 * rc + 128, 512 * fb:512 * fb + 512],
                            STG[:, 512 * sl:512 * sl + 512], waits=[te],
                            chan="stg%d" % sl)
                store_last["V"] = tdo
                if fb == 0:
                    tdo = p.dma("gpsimd", vcam_own.ap()[128 * rc:128 * rc + 128, :],
                                STG[:, 512 * sl:512 * sl + 128], waits=[te],
                                chan="stg%d" % sl)
                    cam_store_last = tdo
                stg_busy[sl] = tdo
        wbf_busy[wslot] = last_T

        if fbi == 1:
            p.cc(lambda e: e.collective_compute(
                     "AllGather", ALU.bypass, replica_groups=cfg["groups"],
                     ins=[kcam_own.ap().opt()], outs=[kcam_ag.ap().opt()]),
                 waits=stg_now(), chan="cc_kcam")
            p.cc(lambda e: e.collective_compute(
                     "AllGather", ALU.bypass, replica_groups=cfg["groups"],
                     ins=[vcam_own.ap().opt()], outs=[vcam_ag.ap().opt()]),
                 waits=(), chan="cc_vcam")
        if sec == "K" and fb == nFB - 1:
            p.cc(lambda e: e.collective_compute(
                     "AllGather", ALU.bypass, replica_groups=cfg["groups"],
                     ins=[kT_own.ap().opt()], outs=[kT_ag.ap().opt()]),
                 waits=stg_now(), chan="cc_k")
        if sec == "V" and fb == nFB - 1:
            p.cc(lambda e: e.collective_compute(
                     "AllGather", ALU.bypass, replica_groups=cfg["groups"],
                     ins=[vR_own.ap().opt()], outs=[vR_ag.ap().opt()]),
                 waits=stg_now(), chan="cc_v")

    t_p1_T = p.now("T")

    pT_of = 0
    qt_of = nT * 512
    KCAM = lambda t: Bb[:, kcam_of + 128 * t:kcam_of + 128 * t + 128]
    VCAM = lambda t: Bb[:, vcam_of + 128 * t:vcam_of + 128 * t + 128]

    t_kcam = p.dma("sync", Bb[:, kcam_of:kcam_of + R], kcam_ag.ap()[0:128, :],
                   waits=[p.now("cc_kcam"), t_p1_T], chan="camld")
    p.dma("sync", Bb[:, kcam_of + R:kcam_of + 2 * R], kcam_ag.ap()[128:256, :],
          waits=[], chan="camld")
    t_cam_ld = p.dma("sync",
                     Bb[:, vcam_of:vcam_of + nT * 128]
                     .rearrange("p (t d) -> p t d", d=128),
                     vcam_ag.ap()[:, :].rearrange("(t p) d -> p t d", p=128),
                     waits=[p.now("cc_vcam")], chan="camld")

    ktile_busy = [t_p1_T, t_p1_T]
    vtile_busy = [t_p1_T, t_p1_T]
    qt_busy = t_p1_T
    prev_ctx_T = t_p1_T
    prev_ctx_V = None
    cstg_dma_busy = None
    tcs_list = []

    ctx_passes = []
    e0 = 0
    while e0 < nD:
        e1 = min(e0 + 6, nD)
        ctx_passes.append((e0, e1))
        e0 = e1
    nVB = 512 // 128

    ONECF = TSQ2[:, 0:1]
    t_onecf = vop(lambda e: e.memset(TSQ2[:, 0:1], 1.0))
    t_mxk = vop(lambda e: e.memset(MS[0:1, MS_tmp:MS_tmp + 1], 0.0), waits=[t_lw])
    for pc in range(nT * 128 // 512):
        tsq_ = vop(lambda e, pc=pc:
                   e.tensor_tensor(M2[:, 0:512],
                                   Bb[:, kcam_of + 512 * pc:kcam_of + 512 * pc + 512],
                                   Bb[:, kcam_of + 512 * pc:kcam_of + 512 * pc + 512],
                                   op=ALU.mult),
                   waits=[t_cam_ld, t_mxk])
        tns = mm(PS[6][0:1, 0:512], ONECF, M2[:, 0:512], True, True,
                 waits=[tsq_, t_onecf, ps_busy[6], t_p1_T], inc="T")
        trd = vop(lambda e: e.reduce_max(MS[0:1, MS_tmp2:MS_tmp2 + 1],
                                         PS[6][0:1, 0:512], axis=AX.X), waits=[tns])
        ps_busy[6] = trd
        t_mxk = vop(lambda e: e.tensor_tensor(MS[0:1, MS_tmp:MS_tmp + 1],
                                              MS[0:1, MS_tmp:MS_tmp + 1],
                                              MS[0:1, MS_tmp2:MS_tmp2 + 1],
                                              op=ALU.max), waits=[trd, t_mxk])
    t_mxk = act(MS[0:1, MS_tmp:MS_tmp + 1], MS[0:1, MS_tmp:MS_tmp + 1], AF.Sqrt,
                waits=[t_mxk])
    t_camK = t_mxk



    for g in range(nG):
        tq = p.dma("sync",
                   A[:, qt_of:qt_of + nD * 512],
                   qT_hbm[128 * g:128 * g + 128, :],
                   waits=[qt_busy] + stg_now(), chan="qt")
        QT = lambda c: A[:, qt_of + 512 * c:qt_of + 512 * c + 512]

        tsq_ = vop(lambda e: e.tensor_tensor(M2[:, 0:512], QT(0), QT(0),
                                             op=ALU.mult),
                   waits=[tq, prev_ctx_V])
        tns = mm(PS[7][0:1, 0:512], ONECF, M2[:, 0:512], True, True,
                 waits=[tsq_, ps_busy[7], t_onecf], inc="T")
        tqn = act(ROWA[0:1, :], PS[7][0:1, 0:512], AF.Sqrt, waits=[tns])
        ps_busy[7] = tqn
        tcr = vop(lambda e: e.tensor_scalar(ROWA[0:1, :], ROWA[0:1, :],
                                            MS[0:1, MS_tmp:MS_tmp + 1], -1.0,
                                            op0=ALU.mult, op1=ALU.mult),
                  waits=[tqn, t_camK])
        t_negC = vop(lambda e: e.tensor_copy(ROWB[0:1, 0:512], ROWA[0:1, :]),
                     waits=[tcr])

        t_cexps = []
        for t in range(nT):
            bank = 6 + (t % 2)
            mm(PS[bank][:, 0:512], KCAM(t), QT(0), True, False,
               waits=[t_cam_ld, tq, ps_busy[bank], t_ld0])
            tmm = mm(PS[bank][:, 0:512], ONEROWB[0:1, :], ROWB[0:1, 0:512],
                     False, True, waits=[t_negC], inc="T")
            tex = act(A[:, pT_of + 512 * t:pT_of + 512 * t + 512],
                      PS[bank][:, 0:512], AF.Exp, scale=SCALE,
                      waits=[tmm, prev_ctx_T])
            ps_busy[bank] = tex
            t_cexps.append(tex)

        t_evs = []
        for tb in range(nT // 2):
            kslot = tb % 2
            koff = ktb_of + kslot * 2 * nD * 128
            tkd = p.dma("sync",
                        Bb[:, koff:koff + 2 * nD * 128]
                        .rearrange("p (a q) -> p a q", a=2),
                        kT_ag.ap()[256 * tb:256 * tb + 256, :]
                        .rearrange("(a p) q -> p a q", p=128),
                        waits=[ktile_busy[kslot],
                               p.now("cc_k") if k_prefetch else t_p1_T,
                               p.now("cc_k")],
                        chan="kt%d" % kslot)
            tmm = None
            for tt in range(2):
                t = 2 * tb + tt
                bank = t % 6
                for c in range(nD):
                    tmm = mm(PS[bank][:, 0:512],
                             Bb[:, koff + tt * nD * 128 + 128 * c:
                                   koff + tt * nD * 128 + 128 * c + 128],
                             QT(c), start=(c == 0), stop=(c == nD - 1),
                             waits=([tkd, ps_busy[bank]] if c == 0 else ()),
                             inc=("T" if c == nD - 1 else None))
                te = act(S16[:, 512 * t:512 * t + 512], PS[bank][:, 0:512],
                         scale=SCALE, waits=[tmm, prev_ctx_V, t_p1_T])
                ps_busy[bank] = te
                t_evs.append(te)
            ktile_busy[kslot] = tmm
        t_score_T = tmm

        t_sum = None
        for t in range(nT):
            t_sum = mm(PS[6][0:1, 0:512], ONECB[:],
                       A[:, pT_of + 512 * t:pT_of + 512 * t + 512],
                       start=(t == 0), stop=(t == nT - 1),
                       waits=([t_cexps[t], ps_busy[6]] if t == 0 else [t_cexps[t]]),
                       inc=("T" if t == nT - 1 else None))
        trc = vop(lambda e: e.reciprocal(ROWA[0:1, :], PS[6][0:1, 0:512]),
                  waits=[t_sum])
        ps_busy[6] = trc
        t_reccols = []
        for j in range(nQC):
            tmj = mm(PS[7][:, 4 + j:5 + j], ROWA[0:1, 128 * j:128 * j + 128],
                     ONEROW[0:1, 0:1], True, True, waits=[trc, ps_busy[7]], inc="T")
            t_reccols.append(act(MS[:, MS_rec + j:MS_rec + j + 1],
                                 PS[7][:, 4 + j:5 + j], waits=[tmj]))
            ps_busy[7] = t_reccols[-1]
        t_tst_done = None
        for j in range(nQC):
            tcm = None
            for t in range(nT):
                tcm = mm(PS[6 + (j % 2)][:, 0:128],
                         A[:, pT_of + 512 * t + 128 * j:pT_of + 512 * t + 128 * j + 128],
                         VCAM(t), start=(t == 0), stop=(t == nT - 1),
                         waits=([t_cexps[-1], ps_busy[6 + (j % 2)], t_cam_ld]
                                if t == 0 else ()),
                         inc=("T" if t == nT - 1 else None))
            t1 = vop(lambda e, j=j:
                     e.scalar_tensor_tensor(TSQ[:], PS[6 + (j % 2)][:, 0:128],
                                            MS[:, MS_rec + j:MS_rec + j + 1],
                                            G[:], op0=ALU.mult, op1=ALU.mult),
                     waits=[tcm, t_reccols[j], t_G])
            ps_busy[6 + (j % 2)] = t1
            t3 = act(TSQB[:], TSQ[:], AF.Tanh, waits=[t1])
            ps6b = PS[6 + (j % 2)][:].bitcast(BF16)
            tt = tr(ps6b[:, 256:384], TSQB[:], IDb[:], waits=[t3, ps_busy[6 + (j % 2)]])
            t5 = act(TST[:, 128 * j:128 * j + 128], ps6b[:, 256:384], waits=[tt])
            ps_busy[6 + (j % 2)] = t5
            t_tst_done = t5

        t_ms2 = vop(lambda e: e.memset(M2[:, 0:512], NEG_BIG), waits=[t_negC])
        t_maxup = t_ms2
        t_adds = []
        for t in range(nT):
            bank = t % 6
            tmm = mm(PS[bank][:, 0:512], KCAM(t), TST[:], True, True,
                     waits=[t_tst_done, ps_busy[bank]], inc="T")
            tad = vop(lambda e, t=t, bank=bank:
                      e.scalar_tensor_tensor(S16[:, 512 * t:512 * t + 512],
                                             PS[bank][:, 0:512],
                                             MS[:, MS_lwS:MS_lwS + 1],
                                             S16[:, 512 * t:512 * t + 512],
                                             op0=ALU.mult, op1=ALU.add),
                      waits=[tmm, t_evs[t]])
            ps_busy[bank] = tad
            t_maxup = vop(lambda e, t=t:
                          e.tensor_tensor(M2[:, 0:512], M2[:, 0:512],
                                          S16[:, 512 * t:512 * t + 512], op=ALU.max),
                          waits=[tad, t_maxup])
            t_adds.append(tad)

        tts = []
        for j in range(nQC):
            tts.append(tr(PS[6][:, 128 * j:128 * j + 128],
                          M2[:, 128 * j:128 * j + 128], IDf[:],
                          waits=[t_maxup, ps_busy[6]]))
        te6 = act(M2[:, 512:1024], PS[6][:, 0:512], waits=[tts[-1]])
        ps_busy[6] = te6
        trs_ = []
        for j in range(nQC):
            trs_.append(vop(lambda e, j=j:
                            e.reduce_max(MS[:, MS_cmax + j:MS_cmax + j + 1],
                                         M2[:, 512 + 128 * j:512 + 128 * j + 128],
                                         axis=AX.X), waits=[te6]))
        tms = None
        for j in range(nQC):
            tms = mm(PS[7][0:1, 128 * j:128 * j + 128],
                     MS[:, MS_cmax + j:MS_cmax + j + 1], IDf[:], True, True,
                     waits=[trs_[j], ps_busy[7]], inc="T")
        t_nrow = act(ROWA[0:1, :], PS[7][0:1, 0:512], scale=-1.0, waits=[tms])
        ps_busy[7] = t_nrow
        tb = mm(PS[6][:, 0:512], ONEROW[0:1, :], ROWA[0:1, :], True, True,
                waits=[t_nrow, ps_busy[6]], inc="T")
        tnb = act(M2[:, 512:1024], PS[6][:, 0:512], waits=[tb])
        ps_busy[6] = tnb
        t_exps = []
        for t in range(nT):
            ts_ = vop(lambda e, t=t:
                      e.tensor_tensor(S16[:, 512 * t:512 * t + 512],
                                      S16[:, 512 * t:512 * t + 512],
                                      M2[:, 512:1024], op=ALU.subtract),
                      waits=[tnb, t_adds[t]])
            tex = act(A[:, pT_of + 512 * t:pT_of + 512 * t + 512],
                      S16[:, 512 * t:512 * t + 512], AF.Exp, waits=[ts_])
            t_exps.append(tex)

        t_rbc = None
        for pi, (pe0, pe1) in enumerate(ctx_passes):
            nbk = pe1 - pe0
            tacc = None
            for vb in range(nT // 4):
                vslot = vb % 2
                voff = vtb_of + vslot * 4 * NBK * 128
                tvd = p.dma("sync",
                            Bb[:, voff:voff + 4 * (pe1 - pe0) * 128]
                            .rearrange("p (a q) -> p a q", a=4),
                            vR_ag.ap()[512 * vb:512 * vb + 512,
                                       128 * pe0:128 * pe1]
                            .rearrange("(a p) q -> p a q", p=128),
                            waits=[vtile_busy[vslot], p.now("cc_v")],
                            chan="vt%d" % vslot)
                for tt in range(4):
                    t = 4 * vb + tt
                    for j in range(nbk):
                        tacc = mm(PS[j][:, 0:512],
                                  Bb[:, voff + (tt * (pe1 - pe0) + j) * 128:
                                        voff + (tt * (pe1 - pe0) + j) * 128 + 128],
                                  A[:, pT_of + 512 * t:pT_of + 512 * t + 512],
                                  start=(t == 0), stop=(t == nT - 1),
                                  waits=([tvd, ps_busy[j], t_exps[t]] if t == 0
                                         else [tvd, t_exps[t]]),
                                  inc=("T" if j == nbk - 1 else None))
                    if pi == 0:
                        t_sum = mm(PS[6][0:1, 0:512], ONECB[:],
                                   A[:, pT_of + 512 * t:pT_of + 512 * t + 512],
                                   start=(t == 0), stop=(t == nT - 1),
                                   waits=([t_exps[t], ps_busy[6]] if t == 0 else ()),
                                   inc=("T" if t == nT - 1 else None))
                vtile_busy[vslot] = tacc
            t_pass_T = tacc
            if pi == 0:
                trc = vop(lambda e: e.reciprocal(ROWA[0:1, :], PS[6][0:1, 0:512]),
                          waits=[t_sum])
                ps_busy[6] = trc
                tbb = mm(PS[7][:, 0:512], ONEROW[0:1, :], ROWA[0:1, :], True, True,
                         waits=[trc, ps_busy[7]], inc="T")
                t_rbc = act(M2[:, 512:1024], PS[7][:, 0:512],
                            waits=[tbb, t_exps[-1]])
                ps_busy[7] = t_rbc
            for j in range(nbk):
                eg = pe0 + j
                tev = vop(lambda e, j=j, eg=eg:
                          e.tensor_tensor(Bb[:, cstg_of + 512 * eg:cstg_of + 512 * eg + 512],
                                          PS[j][:, 0:512], M2[:, 512:1024],
                                          op=ALU.mult),
                          waits=[t_pass_T, t_rbc, cstg_dma_busy])
                ps_busy[j] = tev
        tcs = p.dma("gpsimd",
                    ctxT_hbm[128 * g:128 * g + 128, :],
                    Bb[:, cstg_of:cstg_of + nD * 512],
                    waits=[tev], chan="so_ctx")
        tcs_list.append(tcs)
        cstg_dma_busy = tcs
        prev_ctx_T = t_pass_T
        prev_ctx_V = tev
        qt_busy = t_score_T

    t_p2_T = p.now("T")
    t_p2_V = p.now("V")
    t_p2_Sc = p.now("Sc")

    t_bp = None
    for s in range(D // 512):
        td = p.dma("sync", ROWA[0:1, :], b_p_row[0:1, 512 * s:512 * s + 512],
                   waits=[t_bp if t_bp else t_p2_V, t_p2_Sc], chan="row")
        t_bp = vop(lambda e, s=s: e.tensor_copy(ROWB[0:1, 512 * s:512 * s + 512],
                                                ROWA[0:1, :]),
                   waits=[td, t_p2_V])
    t_ctxl = [None] * nG
    t_wpc = [None] * nD
    for c in range(nD):
        toks = []
        for h in range(max(1, D // 1024)):
            w = min(1024, D - 1024 * h)
            i = c_slot()
            td = p.dma("sync", CS[i][:, 0:w], proj_w[128 * c:128 * c + 128,
                                                     1024 * h:1024 * h + w],
                       waits=[cs_busy[i], t_p2_T if c < 2 else None],
                       chan="cs%d" % i)
            tc = act(Bb[:, D * c + 1024 * h:D * c + 1024 * h + w], CS[i][:, 0:w],
                     waits=[td, t_p2_T if c < 2 else None,
                            cstg_dma_busy if c < 2 else None])
            cs_busy[i] = tc
            toks.append(tc)
        t_wpc[c] = toks[-1]
        if c < nG:
            g = c
            t_ctxl[g] = p.dma("sync",
                              A[:, nD * 512 * g:nD * 512 * (g + 1)],
                              ctxT_hbm[128 * g:128 * g + 128, :],
                              waits=[tcs_list[g], t_p2_T, t_p2_Sc],
                              chan="ctxl%d" % g)

    t_gmax = vop(lambda e: e.memset(MS[:, MS_gmax:MS_gmax + 1], 0.0))
    xo_busy = None
    xres_store = None
    t_xs_stores = []
    t_proj_T = None
    XRESS = [SF[:, 0:D], SF[:, D:2 * D]]
    xr_store = [None, None]
    t1 = None
    for rc in range(nR):
        XRES = XRESS[rc % 2]
        xres_store = xr_store[rc % 2]
        txo = p.dma("sync", Mb[:, 0:D], x_row[128 * rc:128 * rc + 128, :],
                    waits=[xo_busy, t_p2_V if rc == 0 else None], chan="xo")
        t_ln_in = []
        for fs in range(D // 512):
            bank = (rc * (D // 512) + fs) % 6
            ctx_base = nD * 512 * (rc // 4) + 128 * (rc % 4)
            for c in range(nD):
                mm(PS[bank][:, 0:512],
                   A[:, ctx_base + 512 * c:ctx_base + 512 * c + 128],
                   Bb[:, D * c + 512 * fs:D * c + 512 * fs + 512],
                   start=(c == 0), stop=False,
                   waits=[t_wpc[c], t_ctxl[min(rc // 4, nG - 1)]] +
                         ([ps_busy[bank], t_bp] if c == 0 else []))
            tmm = mm(PS[bank][:, 0:512], ONEROWB[0:1, :],
                     ROWB[0:1, 512 * fs:512 * fs + 512],
                     start=False, stop=True, inc="T")
            t_proj_T = tmm
            t1 = vop(lambda e, bank=bank, fs=fs, XRES=XRES:
                     e.tensor_tensor(XRES[:, 512 * fs:512 * fs + 512],
                                     PS[bank][:, 0:512],
                                     Mb[:, 512 * fs:512 * fs + 512], op=ALU.add),
                     waits=[tmm, txo, xres_store,
                            t_gmax if rc > 0 else None,
                            t_p2_Sc if rc == 0 else None,
                            t_p2_V if rc == 0 else None])
            ps_busy[bank] = t1
            t_ln_in.append(t1)
        tbn = []
        for i in range(D // 512):
            tbn.append(vop(lambda e, i=i, XRES=XRES:
                           e.bn_stats(MS[:, MS_bn + 6 * i:MS_bn + 6 * i + 6],
                                      XRES[:, 512 * i:512 * i + 512]),
                           waits=[t_ln_in[i]]))
        tag = vop(lambda e: e.bn_aggr(MS[:, MS_ag:MS_ag + 2],
                                      MS[:, MS_bn:MS_bn + 6 * (D // 512)]),
                  waits=tbn)
        tnm = vop(lambda e: e.tensor_scalar(MS[:, MS_nm:MS_nm + 1],
                                            MS[:, MS_ag:MS_ag + 1], -1.0, None,
                                            op0=ALU.mult), waits=[tag])
        tsq = act(MS[:, MS_rs:MS_rs + 1], MS[:, MS_ag + 1:MS_ag + 2], AF.Sqrt,
                  bias=MS[:, MS_eps:MS_eps + 1], waits=[tag, t_ones])
        trs = vop(lambda e: e.reciprocal(MS[:, MS_rs:MS_rs + 1],
                                         MS[:, MS_rs:MS_rs + 1]), waits=[tsq])
        tln = vop(lambda e, XRES=XRES: e.tensor_scalar(XRES[:, 0:D], XRES[:, 0:D],
                                            MS[:, MS_nm:MS_nm + 1],
                                            MS[:, MS_rs:MS_rs + 1],
                                            op0=ALU.add, op1=ALU.mult),
                  waits=[tnm, trs])
        tso = p.dma("gpsimd", xln1_hbm[128 * rc:128 * rc + 128, :], XRES[:, 0:D],
                    waits=[tln], chan="so_xln%d" % (rc % 2))
        t_xs_stores.append(tso)
        xr_store[rc % 2] = tso
        tam = vop(lambda e, XRES=XRES: e.reduce_max(MS[:, MS_tmp:MS_tmp + 1],
                                         XRES[:, 0:D],
                                         axis=AX.X, apply_absolute_value=True),
                  waits=[tln])
        t_gmax = vop(lambda e: e.tensor_tensor(MS[:, MS_gmax:MS_gmax + 1],
                                               MS[:, MS_gmax:MS_gmax + 1],
                                               MS[:, MS_tmp:MS_tmp + 1], op=ALU.max),
                     waits=[tam, t_gmax])
        xo_busy = t_ln_in[-1]

    tpm = mm(PS[7][0:1, 0:128], MS[:, MS_gmax:MS_gmax + 1], IDf[:], True, True,
             waits=[t_gmax, ps_busy[7]], inc="T")
    tpr = act(ROWA[0:1, 0:128], PS[7][0:1, 0:128], waits=[tpm])
    ps_busy[7] = tpr
    txl = vop(lambda e: e.reduce_max(MS[0:1, MS_tmp:MS_tmp + 1], ROWA[0:1, 0:128],
                                     axis=AX.X), waits=[tpr])
    txl = vop(lambda e: e.tensor_scalar(MS[0:1, MS_tmp:MS_tmp + 1],
                                        MS[0:1, MS_tmp:MS_tmp + 1], 1e-6, None,
                                        op0=ALU.max), waits=[txl])
    txl = vop(lambda e: e.tensor_copy(MS[0:1, 96:97], MS[0:1, MS_tmp:MS_tmp + 1]),
              waits=[txl])
    txsd = p.dma("gpsimd", xs_in.ap()[0:1, 0:16], MS[0:1, 96:112],
                 waits=[txl], chan="so_xs")
    t_cc_xs = p.cc(lambda e: e.collective_compute(
                       "AllReduce", ALU.max, replica_groups=cfg["all_cores"],
                       ins=[xs_in.ap().opt()], outs=[xs_out.ap().opt()]),
                   waits=[txsd], chan="cc_xs")

    txsl = p.dma("sync", MS[0:1, MS_tmp:MS_tmp + 1], xs_out.ap()[0:1, 0:1],
                 waits=[t_cc_xs], chan="xsl")
    npc = max(1, nD // 4)
    t_wl_p = []
    for pcw in range(nD // npc):
        t_wl_p.append(p.dma("gpsimd",
                            A[:, npc * D * pcw:npc * D * (pcw + 1)]
                            .rearrange("p (c n) -> p c n", n=D),
                            ffn_wT[npc * 128 * pcw:npc * 128 * (pcw + 1), :]
                            .rearrange("(c p) n -> p c n", p=128),
                            waits=[t_proj_T], chan="wl%d" % pcw))
    t_wl = t_wl_p[-1]
    t_wm = []
    for c in range(nD):
        t_wm.append(vop(lambda e, c=c:
                        e.reduce_max(MS[:, MS_wmax + c:MS_wmax + c + 1],
                                     A[:, D * c:D * c + D], axis=AX.X,
                                     apply_absolute_value=True),
                        waits=[t_wl_p[c // npc]]))
    t_wm2 = vop(lambda e: e.reduce_max(MS[:, MS_tmp2:MS_tmp2 + 1],
                                       MS[:, MS_wmax:MS_wmax + nD], axis=AX.X),
                waits=[t_wm[-1]])
    tpm = mm(PS[7][0:1, 0:128], MS[:, MS_tmp2:MS_tmp2 + 1], IDf[:], True, True,
             waits=[t_wm2, ps_busy[7]], inc="T")
    tpr = act(ROWA[0:1, 0:128], PS[7][0:1, 0:128], waits=[tpm])
    ps_busy[7] = tpr
    tws = vop(lambda e: e.reduce_max(MS[0:1, MS_tmp2:MS_tmp2 + 1], ROWA[0:1, 0:128],
                                     axis=AX.X), waits=[tpr])
    tws = vop(lambda e: e.tensor_scalar(MS[0:1, MS_tmp2:MS_tmp2 + 1],
                                        MS[0:1, MS_tmp2:MS_tmp2 + 1], 1e-6, None,
                                        op0=ALU.max), waits=[tws])
    t3 = vop(lambda e: e.reciprocal(MS[0:1, MS_wsr:MS_wsr + 1],
                                    MS[0:1, MS_tmp2:MS_tmp2 + 1]), waits=[tws])
    tb3 = mm(PS[7][:, 2:3], ONEROW[0:1, :], MS[0:1, MS_wsr:MS_wsr + 1], True, True,
             waits=[t3, ps_busy[7]], inc="T")
    tcb3 = act(MS[:, MS_wsr:MS_wsr + 1], PS[7][:, 2:3], waits=[tb3])
    ps_busy[7] = tcb3
    t_wq = None
    for c in range(nD):
        half = c % 2
        s8of = (8 + half) * D
        tq1 = act(S8[:, s8of:s8of + D], A[:, D * c:D * c + D],
                  scale=MS[:, MS_wsr:MS_wsr + 1],
                  waits=[tcb3, t_wm[c], t_p2_V if c == 0 else None, t_wq])
        t_wq = vop(lambda e, c=c, s8of=s8of:
                   e.tensor_copy(A[:, D * c:D * c + D], S8[:, s8of:s8of + D]),
                   waits=[tq1, t_wm2 if c == 0 else None])
    t_wq_done = t_wq

    t_fb = None
    for s in range(D // 512):
        td = p.dma("sync", ROWB[0:1, 0:512], ffn_b_row[0:1, 512 * s:512 * s + 512],
                   waits=[t_fb if t_fb else t_bp], chan="row")
        tmb = mm(PS[7][:, 0:512], ONEROWB[0:1, :], ROWB[0:1, 0:512], True, True,
                 waits=[td, ps_busy[7]], inc="T")
        t_fb = act(MbB[:, 512 * s:512 * s + 512], PS[7][:, 0:512],
                   waits=[tmb, xres_store, t_gmax, p.now("V")])
        ps_busy[7] = t_fb
    t_fb_done = t_fb

    t1 = vop(lambda e: e.tensor_tensor(MS[0:1, MS_xsws:MS_xsws + 1],
                                       MS[0:1, MS_tmp:MS_tmp + 1],
                                       MS[0:1, MS_tmp2:MS_tmp2 + 1], op=ALU.mult),
             waits=[tws, txsl])
    t2 = vop(lambda e: e.reciprocal(MS[0:1, MS_xsr:MS_xsr + 1],
                                    MS[0:1, MS_tmp:MS_tmp + 1]), waits=[txsl, t1])
    tb1 = mm(PS[7][:, 0:1], ONEROW[0:1, :], MS[0:1, MS_xsws:MS_xsws + 1], True, True,
             waits=[t1, ps_busy[7]], inc="T")
    tb2 = mm(PS[7][:, 1:2], ONEROW[0:1, :], MS[0:1, MS_xsr:MS_xsr + 1], True, True,
             waits=[t2], inc="T")
    tcb = act(MS[:, MS_xsws:MS_xsws + 1], PS[7][:, 0:1], waits=[tb2])
    tcb2 = act(MS[:, MS_xsr:MS_xsr + 1], PS[7][:, 1:2], waits=[tb2])
    ps_busy[7] = tcb2

    pair_busy = [t_wpc[-1], t_wpc[-1]]
    xq_busy = [None, None]
    zr_busy = [None, None]
    ybf_busy = [None, None]
    tr_prev = None
    out_busy = None
    nQB = max(1, nD // 4)
    zr_base = 2 * D
    tq2s = [None] * nR
    tq1s = [None] * nR

    def emit_quant(rc):
        pair = rc % 2
        CX = C[:, 2048 * pair:2048 * pair + D]
        txd = p.dma("sync", CX[:, :], xln1_hbm[128 * rc:128 * rc + 128, :],
                    waits=[pair_busy[pair],
                           p.now("d_so_xln0") if rc < 2 else None,
                           p.now("d_so_xln1") if rc < 2 else None,
                           xo_busy if rc < 2 else None,
                           t_gmax if rc < 2 else None],
                    chan="xl%d" % pair)
        s8of = (4 + pair) * D
        xqof = pair * D
        tq1 = act(S8[:, s8of:s8of + D], CX[:, :],
                  scale=MS[:, MS_xsr:MS_xsr + 1],
                  waits=[txd, tcb2, t_wq_done if rc == 0 else None,
                         p.now("d_so_xln0") if rc < 2 else None,
                         p.now("d_so_xln1") if rc < 2 else None,
                         t_gmax if rc < 2 else None])
        tq2 = p.op("scalar",
                   lambda e, s8of=s8of, xqof=xqof:
                   e.copy(S16B[:, xqof:xqof + D], S8[:, s8of:s8of + D]),
                   waits=[tq1, xq_busy[pair], t_rbc if rc == 0 else None],
                   inc="Sc")
        tq1s[rc] = tq1
        tq2s[rc] = tq2

    emit_quant(0)
    for rc in range(nR):
        pair = rc % 2
        CX = C[:, 2048 * pair:2048 * pair + D]
        ZR = SF[:, zr_base + pair * D:zr_base + pair * D + D]
        xqof = pair * D
        tq1 = tq1s[rc]
        tq2 = tq2s[rc]
        tevs = []
        for qb in range(nQB):
            nk = min(4, nD)
            psqb = PS[qb][:].bitcast(BF16)
            tts = None
            for k in range(nk):
                kc = qb * 4 + k
                tts = tr(psqb[:, 128 * k:128 * k + 128],
                         S16B[:, xqof + 128 * kc:xqof + 128 * kc + 128], IDb[:],
                         waits=[tq2, ps_busy[qb]])
            tev = p.op("scalar",
                       lambda e, qb=qb, rc=rc, nk=nk, psqb=psqb:
                       e.activation(Bb[:, 0:nD * R].rearrange("p (c r) -> p c r", r=R)
                                    [:, 4 * qb:4 * qb + nk, 128 * rc:128 * rc + 128],
                                    psqb[:, 0:128 * nk]
                                    .rearrange("p (c r) -> p c r", r=128), AF.Copy),
                       waits=[tts, t_proj_T if rc == 0 else None], inc="Sc")
            ps_busy[qb] = tev
            tevs.append(tev)
        tr_prev = tevs[-1]
        xq_busy[pair] = tevs[-1]
        if rc + 1 < nR:
            emit_quant(rc + 1)
        t_z = []
        for fs in range(D // 512):
            bank = 4 + (rc * (D // 512) + fs) % 3
            tmm = None
            for c in range(nD):
                tmm = mm(PS[bank][:, 0:512],
                         Bb[:, R * c + 128 * rc:R * c + 128 * rc + 128],
                         A[:, D * c + 512 * fs:D * c + 512 * fs + 512],
                         start=(c == 0), stop=(c == nD - 1),
                         waits=[tevs[-1] if c == 0 else None, t_wq_done] +
                               ([ps_busy[bank]] if c == 0 else []),
                         inc=("T" if c == nD - 1 else None))
            ybof = D + 512 * (fs % 2)
            ty1 = act(MbB[:, ybof:ybof + 512], PS[bank][:, 0:512],
                      scale=MS[:, MS_xsws:MS_xsws + 1],
                      waits=[tmm, tcb, ybf_busy[fs % 2]])
            ps_busy[bank] = ty1
            ty2 = vop(lambda e, fs=fs, ybof=ybof:
                      e.tensor_tensor(MbB[:, ybof:ybof + 512],
                                      MbB[:, ybof:ybof + 512],
                                      MbB[:, 512 * fs:512 * fs + 512], op=ALU.add),
                      waits=[ty1, t_fb_done])
            ty3 = vop(lambda e, fs=fs, ybof=ybof, CX=CX, ZR=ZR:
                      e.tensor_tensor(ZR[:, 512 * fs:512 * fs + 512],
                                      CX[:, 512 * fs:512 * fs + 512],
                                      MbB[:, ybof:ybof + 512], op=ALU.add),
                      waits=[ty2, tq1, zr_busy[pair],
                             t_gmax if rc < 2 else None])
            ybf_busy[fs % 2] = ty3
            t_z.append(ty3)
        tbn = []
        for i in range(D // 512):
            tbn.append(vop(lambda e, i=i, ZR=ZR:
                           e.bn_stats(MS[:, MS_bn + 6 * i:MS_bn + 6 * i + 6],
                                      ZR[:, 512 * i:512 * i + 512]),
                           waits=[t_z[i]]))
        tag = vop(lambda e: e.bn_aggr(MS[:, MS_ag:MS_ag + 2],
                                      MS[:, MS_bn:MS_bn + 6 * (D // 512)]),
                  waits=tbn)
        tnm = vop(lambda e: e.tensor_scalar(MS[:, MS_nm:MS_nm + 1],
                                            MS[:, MS_ag:MS_ag + 1], -1.0, None,
                                            op0=ALU.mult), waits=[tag])
        tsq = act(MS[:, MS_rs:MS_rs + 1], MS[:, MS_ag + 1:MS_ag + 2], AF.Sqrt,
                  bias=MS[:, MS_eps:MS_eps + 1], waits=[tag])
        trsx = vop(lambda e: e.reciprocal(MS[:, MS_rs:MS_rs + 1],
                                          MS[:, MS_rs:MS_rs + 1]), waits=[tsq])
        tln = vop(lambda e, ZR=ZR: e.tensor_scalar(ZR[:, :], ZR[:, :],
                                                   MS[:, MS_nm:MS_nm + 1],
                                                   MS[:, MS_rs:MS_rs + 1],
                                                   op0=ALU.add, op1=ALU.mult),
                  waits=[tnm, trsx])
        tdo = p.dma("gpsimd", out_hbm[128 * rc:128 * rc + 128, :], ZR[:, :],
                    waits=[tln], chan="out%d" % pair)
        out_busy = tdo
        zr_busy[pair] = tdo
        pair_busy[pair] = t_z[-1]

    p.wait_only("gpsimd", [out_busy, p.now("d_out0"), p.now("d_out1")])
    p.finalize()
    es.close()
    return nc


def stage_inputs(cfg, x_full, attn_w, attn_b, proj_w, proj_b, gate, w0, w1,
                 ffn_w, ffn_b):
    import ml_dtypes
    D, R = cfg["D"], cfg["R"]
    nFS = cfg["nFS"]
    id_f32 = np.eye(128, dtype=np.float32)
    id_bf = np.eye(128, dtype=ml_dtypes.bfloat16)
    attn_b_t = np.ascontiguousarray(attn_b.reshape(nFS, 128).T)
    shared = {
        "attn_w": np.ascontiguousarray(attn_w),
        "attn_b_t": attn_b_t,
        "b_v_row": np.ascontiguousarray(attn_b[2 * D:3 * D][None, :]),
        "proj_w": np.ascontiguousarray(proj_w),
        "b_p_row": np.ascontiguousarray(proj_b[None, :]),
        "gate_row": np.ascontiguousarray(gate[None, :]),
        "w01": np.ascontiguousarray(
            np.concatenate([np.asarray(w0).reshape(1), np.asarray(w1).reshape(1)])[None, :]),
        "ffn_wT": np.ascontiguousarray(ffn_w.T),
        "ffn_b_row": np.ascontiguousarray(ffn_b[None, :]),
        "id_f32": id_f32,
        "id_bf": id_bf,
    }
    maps = []
    for c in range(cfg["n_cores"]):
        b, h = c // 2, c % 2
        xs = x_full[b, h * R:(h + 1) * R, :]
        m = dict(shared)
        m["x_row"] = np.ascontiguousarray(xs)
        m["xT"] = np.ascontiguousarray(xs.T)
        maps.append(m)
    return maps


def kernel(L0_attn_hidden_states, L0_attn_c_attn_w, L0_attn_c_attn_b,
           L0_attn_c_proj_w, L0_attn_c_proj_b, L0_attn_cam_gate,
           L0_attn_cam_w0, L0_attn_cam_w1, L0_ffn_weight, L0_ffn_bias):
    from concourse.bass_utils import run_bass_kernel_spmd
    cfg = make_cfg(D=2048, S=4096, n_cores=8)
    x = np.asarray(L0_attn_hidden_states)
    maps = stage_inputs(cfg, x,
                        np.asarray(L0_attn_c_attn_w), np.asarray(L0_attn_c_attn_b),
                        np.asarray(L0_attn_c_proj_w), np.asarray(L0_attn_c_proj_b),
                        np.asarray(L0_attn_cam_gate), np.asarray(L0_attn_cam_w0),
                        np.asarray(L0_attn_cam_w1), np.asarray(L0_ffn_weight),
                        np.asarray(L0_ffn_bias))
    nc = build(cfg)
    res = run_bass_kernel_spmd(nc, maps, core_ids=list(range(8)))
    R = cfg["R"]
    out = np.empty((4, 4096, 2048), dtype=np.float32)
    for c in range(8):
        b, h = c // 2, c % 2
        out[b, h * R:(h + 1) * R, :] = res.results[c]["out"]
    return out


# version 12 (final)
# speedup vs baseline: 1.0219x; 1.0219x over previous
import contextlib
import numpy as np
import concourse.bass as bass
import concourse.mybir as mybir

F32 = mybir.dt.float32
BF16 = mybir.dt.bfloat16
FP16 = mybir.dt.float16
FP8 = mybir.dt.float8e4
AF = mybir.ActivationFunctionType
AX = mybir.AxisListType
ALU = mybir.AluOpType

NEG_BIG = -3.0e38
SCALE = 1.0 / float(np.sqrt(128.0))
LAYER_DEPTH = 6 / 12


def make_cfg(D=2048, S=4096, n_cores=8):
    c = {}
    c["D"] = D
    c["S"] = S
    c["R"] = S // 2
    c["F"] = 3 * D
    c["nD"] = D // 128
    c["nT"] = S // 128
    c["nR"] = c["R"] // 128
    c["nRB"] = c["R"] // 512
    c["nG"] = c["R"] // 512
    c["nFB"] = D // 512
    c["nFS"] = c["F"] // 128
    c["n_cores"] = n_cores
    c["groups"] = [[i, i + 1] for i in range(0, n_cores, 2)]
    c["all_cores"] = [list(range(n_cores))]
    return c


class Prog:
    ENGS = ("tensor", "vector", "scalar", "gpsimd", "sync")

    def __init__(self, nc, es):
        self.nc = nc
        self.es = es
        self.stream = {e: [] for e in self.ENGS}
        self.sems = {}
        self.cnt = {}
        self.wm = {e: {} for e in self.ENGS}

    def sem(self, *names):
        for n in names:
            self.sems[n] = self.es.enter_context(self.nc.semaphore(n))
            self.cnt[n] = 0

    def now(self, name):
        return (name, self.cnt.get(name, 0))

    def op(self, eng, fn, waits=(), inc=None, n=1):
        ws = []
        for w in waits:
            if w is None:
                continue
            s, v = w
            if v > self.wm[eng].get(s, 0):
                ws.append((s, v))
                self.wm[eng][s] = v
        tok = None
        if inc is not None:
            self.cnt[inc] += n
            tok = (inc, self.cnt[inc])

        def run(e, fn=fn, ws=tuple(ws), inc=inc, n=n):
            for s, v in ws:
                e.wait_ge(self.sems[s], v)
            r = fn(e)
            if inc is not None:
                r.then_inc(self.sems[inc], n)
        self.stream[eng].append(run)
        return tok

    def dma(self, eng, out, in_, waits=(), chan="misc"):
        name = "d_" + chan
        if name not in self.sems:
            self.sem(name)
        return self.op(eng, lambda e, o=out, i=in_: e.dma_start(out=o, in_=i),
                       waits=waits, inc=name, n=16)

    def cc(self, fn, waits=(), chan="cc0"):
        if chan not in self.sems:
            self.sem(chan)
        return self.op("gpsimd", fn, waits=waits, inc=chan, n=1)

    def wait_only(self, eng, toks):
        ws = []
        for w in toks:
            if w is None:
                continue
            s, v = w
            if v > self.wm[eng].get(s, 0):
                ws.append((s, v))
                self.wm[eng][s] = v
        if ws:
            def run(e, ws=tuple(ws)):
                for s, v in ws:
                    e.wait_ge(self.sems[s], v)
            self.stream[eng].append(run)

    def finalize(self):
        block = self.es.enter_context(self.nc.Block())
        st = self.stream

        @block.tensor
        def _(e):
            for f in st["tensor"]:
                f(e)

        @block.vector
        def _(e):
            for f in st["vector"]:
                f(e)

        @block.scalar
        def _(e):
            for f in st["scalar"]:
                f(e)

        @block.gpsimd
        def _(e):
            for f in st["gpsimd"]:
                f(e)

        @block.sync
        def _(e):
            for f in st["sync"]:
                f(e)


def build(cfg):
    D, S, R, F = cfg["D"], cfg["S"], cfg["R"], cfg["F"]
    nD, nT, nR, nRB, nG, nFB, nFS = (cfg[k] for k in
                                     ("nD", "nT", "nR", "nRB", "nG", "nFB", "nFS"))
    nQC = 4

    nc = bass.Bass("TRN2", target_bir_lowering=False)
    es = contextlib.ExitStack()
    p = Prog(nc, es)

    dt = nc.dram_tensor
    x_row = dt("x_row", [R, D], F32, kind="ExternalInput").ap()
    xT = dt("xT", [D, R], F32, kind="ExternalInput").ap()
    attn_w = dt("attn_w", [D, F], F32, kind="ExternalInput").ap()
    attn_b_t = dt("attn_b_t", [128, nFS], F32, kind="ExternalInput").ap()
    b_v_row = dt("b_v_row", [1, D], F32, kind="ExternalInput").ap()
    proj_w = dt("proj_w", [D, D], F32, kind="ExternalInput").ap()
    b_p_row = dt("b_p_row", [1, D], F32, kind="ExternalInput").ap()
    gate_row = dt("gate_row", [1, 128], F32, kind="ExternalInput").ap()
    w01 = dt("w01", [1, 2], F32, kind="ExternalInput").ap()
    ffn_wT = dt("ffn_wT", [D, D], BF16, kind="ExternalInput").ap()
    ffn_b_row = dt("ffn_b_row", [1, D], BF16, kind="ExternalInput").ap()
    id_f32_in = dt("id_f32", [128, 128], F32, kind="ExternalInput").ap()
    id_bf_in = dt("id_bf", [128, 128], BF16, kind="ExternalInput").ap()
    out_hbm = dt("out", [R, D], F32, kind="ExternalOutput").ap()

    qT_hbm = dt("qT_hbm", [nG * 128, nD * 512], BF16).ap()
    kT_own = dt("kT_own", [(nT // 2) * 128, nD * 128], BF16)
    kT_ag = dt("kT_ag", [nT * 128, nD * 128], BF16)
    vR_own = dt("vR_own", [R, D], BF16)
    vR_ag = dt("vR_ag", [S, D], BF16)
    kcam_own = dt("kcam_own", [128, R], BF16)
    vcam_own = dt("vcam_own", [R, 128], BF16)
    kcam_ag = dt("kcam_ag", [256, R], BF16)
    vcam_ag = dt("vcam_ag", [S, 128], BF16)
    xln1_hbm = dt("xln1_hbm", [R, D], F32).ap()
    ctxT_hbm = dt("ctxT_hbm", [nG * 128, nD * 512], BF16).ap()
    xs_in = dt("xs_in", [1, 16], F32)
    xs_out = dt("xs_out", [1, 16], F32)

    A_COLS = max(nD * R, nT * 512 + nD * 512, nD * D)
    WBF_COLS = 2 * nD * 512
    NBK = min(6, nD)
    KTB_SZ = 2 * (2 * nD * 128)
    VTB_SZ = 2 * (4 * NBK * 128)
    CSTG_SZ = nD * 512
    if VTB_SZ + CSTG_SZ <= WBF_COLS:
        vtb_of = 0
        cstg_of = VTB_SZ
        kcam_of = WBF_COLS
        vcam_of = kcam_of + nT * 128
        ktb_of = vcam_of + nT * 128
        k_prefetch = True
        B_COLS = max(ktb_of + KTB_SZ, nD * D, nD * R)
    else:
        kcam_of = WBF_COLS
        vcam_of = kcam_of + nT * 128
        ktb_of = vcam_of + nT * 128
        vtb_of = ktb_of + KTB_SZ
        cstg_of = vtb_of + VTB_SZ
        k_prefetch = False
        B_COLS = max(cstg_of + CSTG_SZ, nD * D, nD * R)

    T = lambda name, shp, d: es.enter_context(nc.sbuf_tensor(name, shp, d))
    A = T("A", [128, A_COLS], BF16)
    Bb = T("Bb", [128, B_COLS], BF16)
    C = T("C", [128, 4096], F32)
    S16 = T("S16", [128, nT * 512], FP16)
    Mb = T("Mb", [128, D], F32)
    M2 = T("M2", [128, 1024], F32)
    MS = T("MS", [128, 128], F32)
    IDf = T("IDf", [128, 128], F32)
    IDb = T("IDb", [128, 128], BF16)
    G = T("G", [128, 128], F32)
    ATB = T("ATB", [128, nFS], F32)
    ROWA = T("ROWA", [1, 512], F32)
    ROWB = T("ROWB", [1, 2048], BF16)
    ONEROW = T("ONEROW", [1, 128], F32)
    ONECB = T("ONECB", [128, 1], BF16)
    ONEROWB = T("ONEROWB", [1, 128], BF16)
    STG = T("STG", [128, 4 * 512], BF16)
    TST = T("TST", [128, 512], BF16)
    TSQ = T("TSQ", [128, 128], F32)
    TSQ2 = T("TSQ2", [128, 128], F32)
    TSQB = T("TSQB", [128, 128], BF16)

    MbB = Mb[:].bitcast(BF16)
    M2B = M2[:].bitcast(BF16)
    S8 = S16[:].bitcast(FP8)
    S16B = S16[:].bitcast(BF16)
    SF = S16[:].bitcast(F32)

    MS_rec = 4
    MS_cmax = 8
    MS_bn = 16
    MS_ag = 40
    MS_nm = 42
    MS_rs = 43
    MS_eps = 44
    MS_gmax = 45
    MS_lw = 46
    MS_lwS = 50
    MS_xsr = 47
    MS_xsws = 48
    MS_wsr = 49
    MS_tmp = 51
    MS_tmp2 = 52
    MS_wmax = 53
    MS_w01 = 90

    PS = [es.enter_context(nc.psum_tensor("ps%d" % i, [128, 512], F32))
          for i in range(8)]

    p.sem("T", "V", "Sc", "Gp")

    ps_busy = [("Sc", 0)] * 8

    def mm(psv, lhsT, rhs, start, stop, waits=(), inc=None):
        return p.op("tensor",
                    lambda e, o=psv, l=lhsT, r=rhs, st=start, sp=stop:
                    e.matmul(o, l, r, start=st, stop=sp),
                    waits=waits, inc=inc)

    def tr(psv, in_, ident, waits=(), inc="T"):
        return p.op("tensor",
                    lambda e, o=psv, i=in_, d=ident: e.transpose(o, i, d),
                    waits=waits, inc=inc)

    def act(out, in_, func=AF.Copy, bias=None, scale=None, waits=(), inc="Sc"):
        if bias is not None and not isinstance(bias, float) and func == AF.Copy:
            func = AF.Identity
        def f(e, out=out, in_=in_, func=func, bias=bias, scale=scale):
            kw = {}
            if bias is not None:
                kw["bias"] = bias
            if scale is not None:
                kw["scale"] = scale
            return e.activation(out, in_, func, **kw)
        return p.op("scalar", f, waits=waits, inc=inc)

    def vop(fn, waits=(), inc="V"):
        return p.op("vector", fn, waits=waits, inc=inc)

    def gop(fn, waits=(), inc="Gp"):
        return p.op("gpsimd", fn, waits=waits, inc=inc)

    CS = [C[:, 1024 * i:1024 * i + 1024] for i in range(4)]
    cs_busy = [None] * 4
    cs_rr = [0]

    def c_slot(allowed=(0, 1, 2, 3)):
        for _ in range(4):
            i = cs_rr[0] % 4
            cs_rr[0] += 1
            if i in allowed:
                return i
        return allowed[0]

    p.dma("sync", IDf[:], id_f32_in, chan="ini")
    p.dma("sync", IDb[:], id_bf_in, chan="ini")
    p.dma("sync", ATB[:], attn_b_t, chan="ini")
    p.dma("sync", ROWA[0:1, 0:128], gate_row, chan="ini")
    t_ld0 = p.dma("sync", MS[0:1, MS_w01:MS_w01 + 2], w01, chan="ini")
    vop(lambda e: e.memset(ONEROW[:], 1.0))
    vop(lambda e: e.memset(ONECB[:], 1.0))
    vop(lambda e: e.memset(ONEROWB[:], 1.0))
    vop(lambda e: e.memset(MS[0:1, 96:112], 0.0))
    t_ones = vop(lambda e: e.memset(MS[:, MS_eps:MS_eps + 1], 1e-5))
    t = mm(PS[0][:, 0:128], ONEROW[0:1, :], ROWA[0:1, 0:128], True, True,
           waits=[t_ld0, t_ones], inc="T")
    t_G = act(G[:], PS[0][:, 0:128], waits=[t])
    ps_busy[0] = t_G
    t = vop(lambda e: e.tensor_scalar(MS[0:1, MS_tmp:MS_tmp + 1],
                                      MS[0:1, MS_w01 + 1:MS_w01 + 2], LAYER_DEPTH,
                                      None, op0=ALU.mult), waits=[t_ld0])
    t = vop(lambda e: e.tensor_tensor(MS[0:1, MS_tmp:MS_tmp + 1],
                                      MS[0:1, MS_tmp:MS_tmp + 1],
                                      MS[0:1, MS_w01:MS_w01 + 1], op=ALU.add),
            waits=[t])
    t = act(MS[0:1, MS_tmp:MS_tmp + 1], MS[0:1, MS_tmp:MS_tmp + 1], AF.Sigmoid,
            waits=[t])
    t = mm(PS[0][:, 128:129], ONEROW[0:1, :], MS[0:1, MS_tmp:MS_tmp + 1], True,
           True, waits=[t, ps_busy[0]], inc="T")
    t_lw = act(MS[:, MS_lw:MS_lw + 1], PS[0][:, 128:129], waits=[t])
    ps_busy[0] = t_lw
    t_lw = vop(lambda e: e.tensor_scalar(MS[:, MS_lwS:MS_lwS + 1],
                                         MS[:, MS_lw:MS_lw + 1], SCALE, None,
                                         op0=ALU.mult), waits=[t_lw])

    t_prev = None
    for s in range(D // 512):
        td = p.dma("sync", ROWA[0:1, :], b_v_row[0:1, 512 * s:512 * s + 512],
                   waits=[t_G if s == 0 else t_prev], chan="row")
        t_prev = vop(lambda e, s=s: e.tensor_copy(ROWB[0:1, 512 * s:512 * s + 512],
                                                  ROWA[0:1, :]), waits=[td])
    t_bv = t_prev

    nxh = (R + 1023) // 1024
    t_xc = [None] * nD
    for c in range(nD):
        for h in range(nxh):
            w = min(1024, R - 1024 * h)
            i = c_slot()
            td = p.dma("sync", CS[i][:, 0:w],
                       xT[128 * c:128 * c + 128, 1024 * h:1024 * h + w],
                       waits=[cs_busy[i]], chan="cs%d" % i)
            tc = vop(lambda e, c=c, h=h, w=w, i=i:
                     e.tensor_copy(A[:, R * c + 1024 * h:R * c + 1024 * h + w],
                                   CS[i][:, 0:w]), waits=[td])
            cs_busy[i] = tc
        t_xc[c] = tc

    sched = [("K", 0), ("V", 0)]
    sched += [("K", i) for i in range(1, nFB)]
    sched += [("V", i) for i in range(1, nFB)]
    sched += [("Q", i) for i in range(nFB)]
    sec_base = {"Q": 0, "K": D, "V": 2 * D}

    WBF = [Bb[:, 0:nD * 512], Bb[:, nD * 512:2 * nD * 512]]
    wbf_busy = [None, None]
    stg_busy = [None] * 4
    store_last = {"K": None, "V": None, "Q": None}
    cam_store_last = None

    def stg_now():
        return [p.now("d_stg%d" % i) for i in range(4) if "d_stg%d" % i in p.sems]

    WSF = [SF[:, 1024 * i:1024 * i + 1024] for i in range(min(8, (2 * S) // 1024))]
    nws = len(WSF)
    wsf_busy = [None] * nws
    wsf_rr = [0]
    wq = 0
    for fbi, (sec, fb) in enumerate(sched):
        fglob = sec_base[sec] + 512 * fb
        wslot = fbi % 2
        t_wc = [None] * nD
        for pc in range(nD // 2):
            d0 = pc * 256
            i = wsf_rr[0] % nws
            wsf_rr[0] += 1
            td = p.dma("sync", WSF[i][:, 0:1024].rearrange("p (a f) -> p a f", f=512),
                       attn_w[d0:d0 + 256, fglob:fglob + 512]
                       .rearrange("(a p) f -> p a f", p=128),
                       waits=[wsf_busy[i]], chan="ws%d" % i)
            tc = act(WBF[wslot][:, 1024 * pc:1024 * pc + 1024], WSF[i][:, 0:1024],
                     waits=[td, wbf_busy[wslot]])
            wsf_busy[i] = tc
            t_wc[2 * pc] = t_wc[2 * pc + 1] = tc

        last_T = None
        if sec in ("K", "Q"):
            for rb in range(nRB):
                for fs in range(4):
                    bank = (rb * 4 + fs) % 6
                    tmm = None
                    for c in range(nD):
                        tmm = mm(PS[bank][:, 0:512],
                                 WBF[wslot][:, 512 * c + 128 * fs:512 * c + 128 * fs + 128],
                                 A[:, R * c + 512 * rb:R * c + 512 * rb + 512],
                                 start=(c == 0), stop=(c == nD - 1),
                                 waits=[t_wc[c], t_xc[c]] +
                                       ([ps_busy[bank]] if c == 0 else []),
                                 inc=("T" if c == nD - 1 else None))
                    last_T = tmm
                    sl = wq % 4
                    wq += 1
                    bias_col = (fglob + 128 * fs) // 128
                    te = act(STG[:, 512 * sl:512 * sl + 512], PS[bank][:, 0:512],
                             bias=ATB[:, bias_col:bias_col + 1],
                             waits=[tmm, stg_busy[sl]])
                    ps_busy[bank] = te
                    if sec == "Q":
                        cg = fb * 4 + fs
                        tdo = p.dma("gpsimd",
                                    qT_hbm[128 * rb:128 * rb + 128,
                                           512 * cg:512 * cg + 512],
                                    STG[:, 512 * sl:512 * sl + 512],
                                    waits=[te], chan="stg%d" % sl)
                        store_last["Q"] = tdo
                    else:
                        ktile0 = rb * 4
                        cg = fb * 4 + fs
                        outap = kT_own.ap() \
                            .rearrange("(t p) q -> t p q", p=128)[ktile0:ktile0 + 4,
                                                                  :, 128 * cg:128 * cg + 128] \
                            .rearrange("t p k -> p t k")
                        inap = STG[:, 512 * sl:512 * sl + 512] \
                            .rearrange("p (t k) -> p t k", k=128)
                        tdo = p.dma("gpsimd", outap, inap, waits=[te], chan="stg%d" % sl)
                        store_last["K"] = tdo
                        if fb == 0 and fs == 0:
                            tdo = p.dma("gpsimd", kcam_own.ap()[:, 512 * rb:512 * rb + 512],
                                        STG[:, 512 * sl:512 * sl + 512],
                                        waits=[te], chan="stg%d" % sl)
                            cam_store_last = tdo
                    stg_busy[sl] = tdo
        else:
            for rc in range(nR):
                bank = rc % 6
                for c in range(nD):
                    mm(PS[bank][:, 0:512],
                       A[:, R * c + 128 * rc:R * c + 128 * rc + 128],
                       WBF[wslot][:, 512 * c:512 * c + 512],
                       start=(c == 0), stop=False,
                       waits=[t_wc[c], t_xc[c]] +
                             ([ps_busy[bank], t_bv] if c == 0 else []))
                tmm = mm(PS[bank][:, 0:512], ONEROWB[0:1, :],
                         ROWB[0:1, 512 * fb:512 * fb + 512],
                         start=False, stop=True, inc="T")
                last_T = tmm
                sl = wq % 4
                wq += 1
                te = act(STG[:, 512 * sl:512 * sl + 512], PS[bank][:, 0:512],
                         waits=[tmm, stg_busy[sl]])
                ps_busy[bank] = te
                tdo = p.dma("gpsimd",
                            vR_own.ap()[128 * rc:128 * rc + 128, 512 * fb:512 * fb + 512],
                            STG[:, 512 * sl:512 * sl + 512], waits=[te],
                            chan="stg%d" % sl)
                store_last["V"] = tdo
                if fb == 0:
                    tdo = p.dma("gpsimd", vcam_own.ap()[128 * rc:128 * rc + 128, :],
                                STG[:, 512 * sl:512 * sl + 128], waits=[te],
                                chan="stg%d" % sl)
                    cam_store_last = tdo
                stg_busy[sl] = tdo
        wbf_busy[wslot] = last_T

        if fbi == 1:
            p.cc(lambda e: e.collective_compute(
                     "AllGather", ALU.bypass, replica_groups=cfg["groups"],
                     ins=[kcam_own.ap().opt()], outs=[kcam_ag.ap().opt()]),
                 waits=stg_now(), chan="cc_kcam")
            p.cc(lambda e: e.collective_compute(
                     "AllGather", ALU.bypass, replica_groups=cfg["groups"],
                     ins=[vcam_own.ap().opt()], outs=[vcam_ag.ap().opt()]),
                 waits=(), chan="cc_vcam")
        if sec == "K" and fb == nFB - 1:
            p.cc(lambda e: e.collective_compute(
                     "AllGather", ALU.bypass, replica_groups=cfg["groups"],
                     ins=[kT_own.ap().opt()], outs=[kT_ag.ap().opt()]),
                 waits=stg_now(), chan="cc_k")
        if sec == "V" and fb == nFB - 1:
            p.cc(lambda e: e.collective_compute(
                     "AllGather", ALU.bypass, replica_groups=cfg["groups"],
                     ins=[vR_own.ap().opt()], outs=[vR_ag.ap().opt()]),
                 waits=stg_now(), chan="cc_v")

    t_p1_T = p.now("T")

    pT_of = 0
    qt_of = nT * 512
    qt_dbl = (A_COLS >= nT * 512 + 2 * nD * 512)
    KCAM = lambda t: Bb[:, kcam_of + 128 * t:kcam_of + 128 * t + 128]
    VCAM = lambda t: Bb[:, vcam_of + 128 * t:vcam_of + 128 * t + 128]

    t_kcam = p.dma("sync", Bb[:, kcam_of:kcam_of + R], kcam_ag.ap()[0:128, :],
                   waits=[p.now("cc_kcam"), t_p1_T], chan="camld")
    p.dma("sync", Bb[:, kcam_of + R:kcam_of + 2 * R], kcam_ag.ap()[128:256, :],
          waits=[], chan="camld")
    t_cam_ld = p.dma("sync",
                     Bb[:, vcam_of:vcam_of + nT * 128]
                     .rearrange("p (t d) -> p t d", d=128),
                     vcam_ag.ap()[:, :].rearrange("(t p) d -> p t d", p=128),
                     waits=[p.now("cc_vcam")], chan="camld")

    ktile_busy = [t_p1_T, t_p1_T]
    vtile_busy = [t_p1_T, t_p1_T]
    qt_busy = t_p1_T
    qt_busy_prev = t_p1_T
    prev_ctx_T = t_p1_T
    prev_ctx_V = None
    cstg_dma_busy = None
    tcs_list = []

    ctx_passes = []
    e0 = 0
    while e0 < nD:
        e1 = min(e0 + 6, nD)
        ctx_passes.append((e0, e1))
        e0 = e1
    nVB = 512 // 128

    ONECF = TSQ2[:, 0:1]
    t_onecf = vop(lambda e: e.memset(TSQ2[:, 0:1], 1.0))
    t_mxk = vop(lambda e: e.memset(MS[0:1, MS_tmp:MS_tmp + 1], 0.0), waits=[t_lw])
    for pc in range(nT * 128 // 512):
        tsq_ = vop(lambda e, pc=pc:
                   e.tensor_tensor(M2[:, 0:512],
                                   Bb[:, kcam_of + 512 * pc:kcam_of + 512 * pc + 512],
                                   Bb[:, kcam_of + 512 * pc:kcam_of + 512 * pc + 512],
                                   op=ALU.mult),
                   waits=[t_cam_ld, t_mxk])
        tns = mm(PS[6][0:1, 0:512], ONECF, M2[:, 0:512], True, True,
                 waits=[tsq_, t_onecf, ps_busy[6], t_p1_T], inc="T")
        trd = vop(lambda e: e.reduce_max(MS[0:1, MS_tmp2:MS_tmp2 + 1],
                                         PS[6][0:1, 0:512], axis=AX.X), waits=[tns])
        ps_busy[6] = trd
        t_mxk = vop(lambda e: e.tensor_tensor(MS[0:1, MS_tmp:MS_tmp + 1],
                                              MS[0:1, MS_tmp:MS_tmp + 1],
                                              MS[0:1, MS_tmp2:MS_tmp2 + 1],
                                              op=ALU.max), waits=[trd, t_mxk])
    t_mxk = act(MS[0:1, MS_tmp:MS_tmp + 1], MS[0:1, MS_tmp:MS_tmp + 1], AF.Sqrt,
                waits=[t_mxk])
    t_camK = t_mxk

    ws_early = False
    t_wm = [None] * nD
    WSP = Bb[:, cstg_of + CSTG_SZ:cstg_of + CSTG_SZ + D] if ws_early else None

    def emit_wsp(cs):
        for c in cs:
            twp_d = p.dma("gpsimd", WSP[:, :], ffn_wT[128 * c:128 * c + 128, :],
                          waits=[t_p1_T if c == 0 else t_wm[c - 1]], chan="wsp")
            t_wm[c] = vop(lambda e, c=c:
                          e.reduce_max(MS[:, MS_wmax + c:MS_wmax + c + 1],
                                       WSP[:, :], axis=AX.X,
                                       apply_absolute_value=True), waits=[twp_d])




    for g in range(nG):
        qof = qt_of + (g % 2) * nD * 512 if qt_dbl else qt_of
        tq = p.dma("sync",
                   A[:, qof:qof + nD * 512],
                   qT_hbm[128 * g:128 * g + 128, :],
                   waits=[qt_busy if (not qt_dbl or g >= 2) else t_p1_T] + stg_now(),
                   chan="qt%d" % (g % 2 if qt_dbl else 0))
        QT = lambda c, qof=qof: A[:, qof + 512 * c:qof + 512 * c + 512]

        tsq_ = vop(lambda e: e.tensor_tensor(M2[:, 0:512], QT(0), QT(0),
                                             op=ALU.mult),
                   waits=[tq, prev_ctx_V])
        tns = mm(PS[7][0:1, 0:512], ONECF, M2[:, 0:512], True, True,
                 waits=[tsq_, ps_busy[7], t_onecf], inc="T")
        tqn = act(ROWA[0:1, :], PS[7][0:1, 0:512], AF.Sqrt, waits=[tns])
        ps_busy[7] = tqn
        tcr = vop(lambda e: e.tensor_scalar(ROWA[0:1, :], ROWA[0:1, :],
                                            MS[0:1, MS_tmp:MS_tmp + 1], -1.0,
                                            op0=ALU.mult, op1=ALU.mult),
                  waits=[tqn, t_camK])
        t_negC = vop(lambda e: e.tensor_copy(ROWB[0:1, 0:512], ROWA[0:1, :]),
                     waits=[tcr])

        t_cexps = []
        for t in range(nT):
            bank = 6 + (t % 2)
            mm(PS[bank][:, 0:512], KCAM(t), QT(0), True, False,
               waits=[t_cam_ld, tq, ps_busy[bank], t_ld0])
            tmm = mm(PS[bank][:, 0:512], ONEROWB[0:1, :], ROWB[0:1, 0:512],
                     False, True, waits=[t_negC], inc="T")
            tex = act(A[:, pT_of + 512 * t:pT_of + 512 * t + 512],
                      PS[bank][:, 0:512], AF.Exp, scale=SCALE,
                      waits=[tmm, prev_ctx_T])
            ps_busy[bank] = tex
            t_cexps.append(tex)

        t_evs = []
        for tb in range(nT // 2):
            kslot = tb % 2
            koff = ktb_of + kslot * 2 * nD * 128
            tkd = p.dma("sync",
                        Bb[:, koff:koff + 2 * nD * 128]
                        .rearrange("p (a q) -> p a q", a=2),
                        kT_ag.ap()[256 * tb:256 * tb + 256, :]
                        .rearrange("(a p) q -> p a q", p=128),
                        waits=[ktile_busy[kslot],
                               p.now("cc_k") if k_prefetch else t_p1_T,
                               p.now("cc_k")],
                        chan="kt%d" % kslot)
            tmm = None
            for tt in range(2):
                t = 2 * tb + tt
                bank = t % 6
                for c in range(nD):
                    tmm = mm(PS[bank][:, 0:512],
                             Bb[:, koff + tt * nD * 128 + 128 * c:
                                   koff + tt * nD * 128 + 128 * c + 128],
                             QT(c), start=(c == 0), stop=(c == nD - 1),
                             waits=([tkd, ps_busy[bank]] if c == 0 else ()),
                             inc=("T" if c == nD - 1 else None))
                te = act(S16[:, 512 * t:512 * t + 512], PS[bank][:, 0:512],
                         scale=SCALE, waits=[tmm, prev_ctx_V, t_p1_T])
                ps_busy[bank] = te
                t_evs.append(te)
            ktile_busy[kslot] = tmm
        t_score_T = tmm

        t_sum = None
        for t in range(nT):
            t_sum = mm(PS[6][0:1, 0:512], ONECB[:],
                       A[:, pT_of + 512 * t:pT_of + 512 * t + 512],
                       start=(t == 0), stop=(t == nT - 1),
                       waits=([t_cexps[t], ps_busy[6]] if t == 0 else [t_cexps[t]]),
                       inc=("T" if t == nT - 1 else None))
        trc = vop(lambda e: e.reciprocal(ROWA[0:1, :], PS[6][0:1, 0:512]),
                  waits=[t_sum])
        ps_busy[6] = trc
        t_reccols = []
        for j in range(nQC):
            tmj = mm(PS[7][:, 4 + j:5 + j], ROWA[0:1, 128 * j:128 * j + 128],
                     ONEROW[0:1, 0:1], True, True, waits=[trc, ps_busy[7]], inc="T")
            t_reccols.append(act(MS[:, MS_rec + j:MS_rec + j + 1],
                                 PS[7][:, 4 + j:5 + j], waits=[tmj]))
            ps_busy[7] = t_reccols[-1]
        t_tst_done = None
        for j in range(nQC):
            tcm = None
            for t in range(nT):
                tcm = mm(PS[6 + (j % 2)][:, 0:128],
                         A[:, pT_of + 512 * t + 128 * j:pT_of + 512 * t + 128 * j + 128],
                         VCAM(t), start=(t == 0), stop=(t == nT - 1),
                         waits=([t_cexps[-1], ps_busy[6 + (j % 2)], t_cam_ld]
                                if t == 0 else ()),
                         inc=("T" if t == nT - 1 else None))
            t1 = vop(lambda e, j=j:
                     e.scalar_tensor_tensor(TSQ[:], PS[6 + (j % 2)][:, 0:128],
                                            MS[:, MS_rec + j:MS_rec + j + 1],
                                            G[:], op0=ALU.mult, op1=ALU.mult),
                     waits=[tcm, t_reccols[j], t_G])
            ps_busy[6 + (j % 2)] = t1
            t3 = act(TSQB[:], TSQ[:], AF.Tanh, waits=[t1])
            ps6b = PS[6 + (j % 2)][:].bitcast(BF16)
            tt = tr(ps6b[:, 256:384], TSQB[:], IDb[:], waits=[t3, ps_busy[6 + (j % 2)]])
            t5 = act(TST[:, 128 * j:128 * j + 128], ps6b[:, 256:384], waits=[tt])
            ps_busy[6 + (j % 2)] = t5
            t_tst_done = t5

        t_ms2 = vop(lambda e: e.memset(M2[:, 0:512], NEG_BIG), waits=[t_negC])
        t_maxup = t_ms2
        t_adds = []
        for t in range(nT):
            bank = t % 6
            tmm = mm(PS[bank][:, 0:512], KCAM(t), TST[:], True, True,
                     waits=[t_tst_done, ps_busy[bank]], inc="T")
            tad = vop(lambda e, t=t, bank=bank:
                      e.scalar_tensor_tensor(S16[:, 512 * t:512 * t + 512],
                                             PS[bank][:, 0:512],
                                             MS[:, MS_lwS:MS_lwS + 1],
                                             S16[:, 512 * t:512 * t + 512],
                                             op0=ALU.mult, op1=ALU.add),
                      waits=[tmm, t_evs[t]])
            ps_busy[bank] = tad
            t_maxup = vop(lambda e, t=t:
                          e.tensor_tensor(M2[:, 0:512], M2[:, 0:512],
                                          S16[:, 512 * t:512 * t + 512], op=ALU.max),
                          waits=[tad, t_maxup])
            t_adds.append(tad)

        tts = []
        for j in range(nQC):
            tts.append(tr(PS[6][:, 128 * j:128 * j + 128],
                          M2[:, 128 * j:128 * j + 128], IDf[:],
                          waits=[t_maxup, ps_busy[6]]))
        te6 = act(M2[:, 512:1024], PS[6][:, 0:512], waits=[tts[-1]])
        ps_busy[6] = te6
        trs_ = []
        for j in range(nQC):
            trs_.append(vop(lambda e, j=j:
                            e.reduce_max(MS[:, MS_cmax + j:MS_cmax + j + 1],
                                         M2[:, 512 + 128 * j:512 + 128 * j + 128],
                                         axis=AX.X), waits=[te6]))
        tms = None
        for j in range(nQC):
            tms = mm(PS[7][0:1, 128 * j:128 * j + 128],
                     MS[:, MS_cmax + j:MS_cmax + j + 1], IDf[:], True, True,
                     waits=[trs_[j], ps_busy[7]], inc="T")
        t_nrow = act(ROWA[0:1, :], PS[7][0:1, 0:512], scale=-1.0, waits=[tms])
        ps_busy[7] = t_nrow
        tb = mm(PS[6][:, 0:512], ONEROW[0:1, :], ROWA[0:1, :], True, True,
                waits=[t_nrow, ps_busy[6]], inc="T")
        tnb = act(M2[:, 512:1024], PS[6][:, 0:512], waits=[tb])
        ps_busy[6] = tnb
        t_exps = []
        for t in range(nT):
            ts_ = vop(lambda e, t=t:
                      e.tensor_tensor(S16[:, 512 * t:512 * t + 512],
                                      S16[:, 512 * t:512 * t + 512],
                                      M2[:, 512:1024], op=ALU.subtract),
                      waits=[tnb, t_adds[t]])
            tex = act(A[:, pT_of + 512 * t:pT_of + 512 * t + 512],
                      S16[:, 512 * t:512 * t + 512], AF.Exp, waits=[ts_])
            t_exps.append(tex)

        t_rbc = None
        for pi, (pe0, pe1) in enumerate(ctx_passes):
            nbk = pe1 - pe0
            tacc = None
            for vb in range(nT // 4):
                vslot = vb % 2
                voff = vtb_of + vslot * 4 * NBK * 128
                tvd = p.dma("sync",
                            Bb[:, voff:voff + 4 * (pe1 - pe0) * 128]
                            .rearrange("p (a q) -> p a q", a=4),
                            vR_ag.ap()[512 * vb:512 * vb + 512,
                                       128 * pe0:128 * pe1]
                            .rearrange("(a p) q -> p a q", p=128),
                            waits=[vtile_busy[vslot], p.now("cc_v")],
                            chan="vt%d" % vslot)
                for tt in range(4):
                    t = 4 * vb + tt
                    for j in range(nbk):
                        tacc = mm(PS[j][:, 0:512],
                                  Bb[:, voff + (tt * (pe1 - pe0) + j) * 128:
                                        voff + (tt * (pe1 - pe0) + j) * 128 + 128],
                                  A[:, pT_of + 512 * t:pT_of + 512 * t + 512],
                                  start=(t == 0), stop=(t == nT - 1),
                                  waits=([tvd, ps_busy[j], t_exps[t]] if t == 0
                                         else [tvd, t_exps[t]]),
                                  inc=("T" if j == nbk - 1 else None))
                    if pi == 0:
                        t_sum = mm(PS[6][0:1, 0:512], ONECB[:],
                                   A[:, pT_of + 512 * t:pT_of + 512 * t + 512],
                                   start=(t == 0), stop=(t == nT - 1),
                                   waits=([t_exps[t], ps_busy[6]] if t == 0 else ()),
                                   inc=("T" if t == nT - 1 else None))
                vtile_busy[vslot] = tacc
            t_pass_T = tacc
            if pi == 0:
                trc = vop(lambda e: e.reciprocal(ROWA[0:1, :], PS[6][0:1, 0:512]),
                          waits=[t_sum])
                ps_busy[6] = trc
                tbb = mm(PS[7][:, 0:512], ONEROW[0:1, :], ROWA[0:1, :], True, True,
                         waits=[trc, ps_busy[7]], inc="T")
                t_rbc = act(M2[:, 512:1024], PS[7][:, 0:512],
                            waits=[tbb, t_exps[-1]])
                ps_busy[7] = t_rbc
            for j in range(nbk):
                eg = pe0 + j
                tev = vop(lambda e, j=j, eg=eg:
                          e.tensor_tensor(Bb[:, cstg_of + 512 * eg:cstg_of + 512 * eg + 512],
                                          PS[j][:, 0:512], M2[:, 512:1024],
                                          op=ALU.mult),
                          waits=[t_pass_T, t_rbc, cstg_dma_busy])
                ps_busy[j] = tev
        tcs = p.dma("gpsimd",
                    ctxT_hbm[128 * g:128 * g + 128, :],
                    Bb[:, cstg_of:cstg_of + nD * 512],
                    waits=[tev], chan="so_ctx")
        tcs_list.append(tcs)
        cstg_dma_busy = tcs
        if ws_early:
            emit_wsp(range(g * nD // nG, (g + 1) * nD // nG))
        prev_ctx_T = t_pass_T
        prev_ctx_V = tev
        if qt_dbl:
            qt_busy = qt_busy_prev
            qt_busy_prev = t_score_T
        else:
            qt_busy = t_score_T

    t_p2_T = p.now("T")
    t_p2_V = p.now("V")
    t_p2_Sc = p.now("Sc")

    t_bp = None
    for s in range(D // 512):
        td = p.dma("sync", ROWA[0:1, :], b_p_row[0:1, 512 * s:512 * s + 512],
                   waits=[t_bp if t_bp else t_p2_V, t_p2_Sc], chan="row")
        t_bp = vop(lambda e, s=s: e.tensor_copy(ROWB[0:1, 512 * s:512 * s + 512],
                                                ROWA[0:1, :]),
                   waits=[td, t_p2_V])
    t_ctxl = [None] * nG
    t_wpc = [None] * nD
    for c in range(nD):
        toks = []
        for h in range(max(1, D // 1024)):
            w = min(1024, D - 1024 * h)
            i = c_slot()
            td = p.dma("sync", CS[i][:, 0:w], proj_w[128 * c:128 * c + 128,
                                                     1024 * h:1024 * h + w],
                       waits=[cs_busy[i]], chan="cs%d" % i)
            tc = act(Bb[:, D * c + 1024 * h:D * c + 1024 * h + w], CS[i][:, 0:w],
                     waits=[td, t_p2_T if c < 2 else None,
                            cstg_dma_busy if c < 2 else None,
                            t_wm[nD - 1] if (ws_early and
                                             D * c + 1024 * h + w >
                                             cstg_of + CSTG_SZ) else None])
            cs_busy[i] = tc
            toks.append(tc)
        t_wpc[c] = toks[-1]
        if c < nG:
            g = c
            t_ctxl[g] = p.dma("sync",
                              A[:, nD * 512 * g:nD * 512 * (g + 1)],
                              ctxT_hbm[128 * g:128 * g + 128, :],
                              waits=[tcs_list[g], t_p2_T, t_p2_Sc],
                              chan="ctxl%d" % g)

    t_gmax = vop(lambda e: e.memset(MS[:, MS_gmax:MS_gmax + 1], 0.0))
    xo_busy = None
    xres_store = None
    t_xs_stores = []
    t_proj_T = None
    XRESS = [SF[:, 0:D], SF[:, D:2 * D]]
    xr_store = [None, None]
    t1 = None
    for rc in range(nR):
        XRES = XRESS[rc % 2]
        xres_store = xr_store[rc % 2]
        txo = p.dma("sync", Mb[:, 0:D], x_row[128 * rc:128 * rc + 128, :],
                    waits=[xo_busy, t_p2_V if rc == 0 else None], chan="xo")
        t_ln_in = []
        for fs in range(D // 512):
            bank = (rc * (D // 512) + fs) % 6
            ctx_base = nD * 512 * (rc // 4) + 128 * (rc % 4)
            for c in range(nD):
                mm(PS[bank][:, 0:512],
                   A[:, ctx_base + 512 * c:ctx_base + 512 * c + 128],
                   Bb[:, D * c + 512 * fs:D * c + 512 * fs + 512],
                   start=(c == 0), stop=False,
                   waits=[t_wpc[c], t_ctxl[min(rc // 4, nG - 1)]] +
                         ([ps_busy[bank], t_bp] if c == 0 else []))
            tmm = mm(PS[bank][:, 0:512], ONEROWB[0:1, :],
                     ROWB[0:1, 512 * fs:512 * fs + 512],
                     start=False, stop=True, inc="T")
            t_proj_T = tmm
            t1 = vop(lambda e, bank=bank, fs=fs, XRES=XRES:
                     e.tensor_tensor(XRES[:, 512 * fs:512 * fs + 512],
                                     PS[bank][:, 0:512],
                                     Mb[:, 512 * fs:512 * fs + 512], op=ALU.add),
                     waits=[tmm, txo, xres_store,
                            t_gmax if rc > 0 else None,
                            t_p2_Sc if rc == 0 else None,
                            t_p2_V if rc == 0 else None])
            ps_busy[bank] = t1
            t_ln_in.append(t1)
        tbn = []
        for i in range(D // 512):
            tbn.append(vop(lambda e, i=i, XRES=XRES:
                           e.bn_stats(MS[:, MS_bn + 6 * i:MS_bn + 6 * i + 6],
                                      XRES[:, 512 * i:512 * i + 512]),
                           waits=[t_ln_in[i]]))
        tag = vop(lambda e: e.bn_aggr(MS[:, MS_ag:MS_ag + 2],
                                      MS[:, MS_bn:MS_bn + 6 * (D // 512)]),
                  waits=tbn)
        tnm = vop(lambda e: e.tensor_scalar(MS[:, MS_nm:MS_nm + 1],
                                            MS[:, MS_ag:MS_ag + 1], -1.0, None,
                                            op0=ALU.mult), waits=[tag])
        tsq = act(MS[:, MS_rs:MS_rs + 1], MS[:, MS_ag + 1:MS_ag + 2], AF.Sqrt,
                  bias=MS[:, MS_eps:MS_eps + 1], waits=[tag, t_ones])
        trs = vop(lambda e: e.reciprocal(MS[:, MS_rs:MS_rs + 1],
                                         MS[:, MS_rs:MS_rs + 1]), waits=[tsq])
        tln = vop(lambda e, XRES=XRES: e.tensor_scalar(XRES[:, 0:D], XRES[:, 0:D],
                                            MS[:, MS_nm:MS_nm + 1],
                                            MS[:, MS_rs:MS_rs + 1],
                                            op0=ALU.add, op1=ALU.mult),
                  waits=[tnm, trs])
        tso = p.dma("gpsimd", xln1_hbm[128 * rc:128 * rc + 128, :], XRES[:, 0:D],
                    waits=[tln], chan="so_xln%d" % (rc % 2))
        t_xs_stores.append(tso)
        xr_store[rc % 2] = tso
        tam = vop(lambda e, XRES=XRES: e.reduce_max(MS[:, MS_tmp:MS_tmp + 1],
                                         XRES[:, 0:D],
                                         axis=AX.X, apply_absolute_value=True),
                  waits=[tln])
        t_gmax = vop(lambda e: e.tensor_tensor(MS[:, MS_gmax:MS_gmax + 1],
                                               MS[:, MS_gmax:MS_gmax + 1],
                                               MS[:, MS_tmp:MS_tmp + 1], op=ALU.max),
                     waits=[tam, t_gmax])
        xo_busy = t_ln_in[-1]

    tpm = mm(PS[7][0:1, 0:128], MS[:, MS_gmax:MS_gmax + 1], IDf[:], True, True,
             waits=[t_gmax, ps_busy[7]], inc="T")
    tpr = act(ROWA[0:1, 0:128], PS[7][0:1, 0:128], waits=[tpm])
    ps_busy[7] = tpr
    txl = vop(lambda e: e.reduce_max(MS[0:1, MS_tmp:MS_tmp + 1], ROWA[0:1, 0:128],
                                     axis=AX.X), waits=[tpr])
    txl = vop(lambda e: e.tensor_scalar(MS[0:1, MS_tmp:MS_tmp + 1],
                                        MS[0:1, MS_tmp:MS_tmp + 1], 1e-6, None,
                                        op0=ALU.max), waits=[txl])
    txl = vop(lambda e: e.tensor_copy(MS[0:1, 96:97], MS[0:1, MS_tmp:MS_tmp + 1]),
              waits=[txl])
    txsd = p.dma("gpsimd", xs_in.ap()[0:1, 0:16], MS[0:1, 96:112],
                 waits=[txl], chan="so_xs")
    t_cc_xs = p.cc(lambda e: e.collective_compute(
                       "AllReduce", ALU.max, replica_groups=cfg["all_cores"],
                       ins=[xs_in.ap().opt()], outs=[xs_out.ap().opt()]),
                   waits=[txsd], chan="cc_xs")

    txsl = p.dma("sync", MS[0:1, MS_tmp:MS_tmp + 1], xs_out.ap()[0:1, 0:1],
                 waits=[t_cc_xs], chan="xsl")
    npc = max(1, nD // 4)
    t_wl_p = []
    for pcw in range(nD // npc):
        t_wl_p.append(p.dma("gpsimd",
                            A[:, npc * D * pcw:npc * D * (pcw + 1)]
                            .rearrange("p (c n) -> p c n", n=D),
                            ffn_wT[npc * 128 * pcw:npc * 128 * (pcw + 1), :]
                            .rearrange("(c p) n -> p c n", p=128),
                            waits=[t_proj_T], chan="wl%d" % pcw))
    t_wl = t_wl_p[-1]
    if not ws_early:
        for c in range(nD):
            t_wm[c] = vop(lambda e, c=c:
                          e.reduce_max(MS[:, MS_wmax + c:MS_wmax + c + 1],
                                       A[:, D * c:D * c + D], axis=AX.X,
                                       apply_absolute_value=True),
                          waits=[t_wl_p[c // npc]])
    t_wm2 = vop(lambda e: e.reduce_max(MS[:, MS_tmp2:MS_tmp2 + 1],
                                       MS[:, MS_wmax:MS_wmax + nD], axis=AX.X),
                waits=[t_wm[-1]])
    tpm = mm(PS[7][0:1, 0:128], MS[:, MS_tmp2:MS_tmp2 + 1], IDf[:], True, True,
             waits=[t_wm2, ps_busy[7]], inc="T")
    tpr = act(ROWA[0:1, 0:128], PS[7][0:1, 0:128], waits=[tpm])
    ps_busy[7] = tpr
    tws = vop(lambda e: e.reduce_max(MS[0:1, MS_tmp2:MS_tmp2 + 1], ROWA[0:1, 0:128],
                                     axis=AX.X), waits=[tpr])
    tws = vop(lambda e: e.tensor_scalar(MS[0:1, MS_tmp2:MS_tmp2 + 1],
                                        MS[0:1, MS_tmp2:MS_tmp2 + 1], 1e-6, None,
                                        op0=ALU.max), waits=[tws])
    t3 = vop(lambda e: e.reciprocal(MS[0:1, MS_wsr:MS_wsr + 1],
                                    MS[0:1, MS_tmp2:MS_tmp2 + 1]), waits=[tws])
    tb3 = mm(PS[7][:, 2:3], ONEROW[0:1, :], MS[0:1, MS_wsr:MS_wsr + 1], True, True,
             waits=[t3, ps_busy[7]], inc="T")
    tcb3 = act(MS[:, MS_wsr:MS_wsr + 1], PS[7][:, 2:3], waits=[tb3])
    ps_busy[7] = tcb3
    t_wq = None
    for c in range(nD):
        half = c % 2
        s8of = (8 + half) * D
        tq1 = act(S8[:, s8of:s8of + D], A[:, D * c:D * c + D],
                  scale=MS[:, MS_wsr:MS_wsr + 1],
                  waits=[tcb3, t_wm[c], t_wl_p[c // npc],
                         t_p2_V if c == 0 else None, t_wq])
        t_wq = vop(lambda e, c=c, s8of=s8of:
                   e.tensor_copy(A[:, D * c:D * c + D], S8[:, s8of:s8of + D]),
                   waits=[tq1, t_wm2 if c == 0 else None])
    t_wq_done = t_wq

    t_fb = None
    for s in range(D // 512):
        td = p.dma("sync", ROWB[0:1, 0:512], ffn_b_row[0:1, 512 * s:512 * s + 512],
                   waits=[t_fb if t_fb else t_bp], chan="row")
        tmb = mm(PS[7][:, 0:512], ONEROWB[0:1, :], ROWB[0:1, 0:512], True, True,
                 waits=[td, ps_busy[7]], inc="T")
        t_fb = act(MbB[:, 512 * s:512 * s + 512], PS[7][:, 0:512],
                   waits=[tmb, xres_store, t_gmax, p.now("V")])
        ps_busy[7] = t_fb
    t_fb_done = t_fb

    t1 = vop(lambda e: e.tensor_tensor(MS[0:1, MS_xsws:MS_xsws + 1],
                                       MS[0:1, MS_tmp:MS_tmp + 1],
                                       MS[0:1, MS_tmp2:MS_tmp2 + 1], op=ALU.mult),
             waits=[tws, txsl])
    t2 = vop(lambda e: e.reciprocal(MS[0:1, MS_xsr:MS_xsr + 1],
                                    MS[0:1, MS_tmp:MS_tmp + 1]), waits=[txsl, t1])
    tb1 = mm(PS[7][:, 0:1], ONEROW[0:1, :], MS[0:1, MS_xsws:MS_xsws + 1], True, True,
             waits=[t1, ps_busy[7]], inc="T")
    tb2 = mm(PS[7][:, 1:2], ONEROW[0:1, :], MS[0:1, MS_xsr:MS_xsr + 1], True, True,
             waits=[t2], inc="T")
    tcb = act(MS[:, MS_xsws:MS_xsws + 1], PS[7][:, 0:1], waits=[tb2])
    tcb2 = act(MS[:, MS_xsr:MS_xsr + 1], PS[7][:, 1:2], waits=[tb2])
    ps_busy[7] = tcb2

    pair_busy = [t_wpc[-1], t_wpc[-1]]
    xq_busy = [None, None]
    zr_busy = [None, None]
    ybf_busy = [None, None]
    tr_prev = None
    out_busy = None
    nQB = max(1, nD // 4)
    zr_base = 2 * D
    tq2s = [None] * nR
    tq1s = [None] * nR

    def emit_quant(rc):
        pair = rc % 2
        CX = C[:, 2048 * pair:2048 * pair + D]
        txd = p.dma("sync", CX[:, :], xln1_hbm[128 * rc:128 * rc + 128, :],
                    waits=[pair_busy[pair],
                           p.now("d_so_xln0") if rc < 2 else None,
                           p.now("d_so_xln1") if rc < 2 else None,
                           xo_busy if rc < 2 else None,
                           t_gmax if rc < 2 else None],
                    chan="xl%d" % pair)
        s8of = (4 + pair) * D
        xqof = pair * D
        tq1 = act(S8[:, s8of:s8of + D], CX[:, :],
                  scale=MS[:, MS_xsr:MS_xsr + 1],
                  waits=[txd, tcb2, t_wq_done if rc == 0 else None,
                         p.now("d_so_xln0") if rc < 2 else None,
                         p.now("d_so_xln1") if rc < 2 else None,
                         t_gmax if rc < 2 else None])
        tq2 = p.op("scalar",
                   lambda e, s8of=s8of, xqof=xqof:
                   e.copy(S16B[:, xqof:xqof + D], S8[:, s8of:s8of + D]),
                   waits=[tq1, xq_busy[pair], t_rbc if rc == 0 else None],
                   inc="Sc")
        tq1s[rc] = tq1
        tq2s[rc] = tq2

    emit_quant(0)
    for rc in range(nR):
        pair = rc % 2
        CX = C[:, 2048 * pair:2048 * pair + D]
        ZR = SF[:, zr_base + pair * D:zr_base + pair * D + D]
        xqof = pair * D
        tq1 = tq1s[rc]
        tq2 = tq2s[rc]
        tevs = []
        for qb in range(nQB):
            nk = min(4, nD)
            psqb = PS[qb][:].bitcast(BF16)
            tts = None
            for k in range(nk):
                kc = qb * 4 + k
                tts = tr(psqb[:, 128 * k:128 * k + 128],
                         S16B[:, xqof + 128 * kc:xqof + 128 * kc + 128], IDb[:],
                         waits=[tq2, ps_busy[qb]])
            tev = p.op("scalar",
                       lambda e, qb=qb, rc=rc, nk=nk, psqb=psqb:
                       e.activation(Bb[:, 0:nD * R].rearrange("p (c r) -> p c r", r=R)
                                    [:, 4 * qb:4 * qb + nk, 128 * rc:128 * rc + 128],
                                    psqb[:, 0:128 * nk]
                                    .rearrange("p (c r) -> p c r", r=128), AF.Copy),
                       waits=[tts, t_proj_T if rc == 0 else None], inc="Sc")
            ps_busy[qb] = tev
            tevs.append(tev)
        tr_prev = tevs[-1]
        xq_busy[pair] = tevs[-1]
        if rc + 1 < nR:
            emit_quant(rc + 1)
        t_z = []
        for fs in range(D // 512):
            bank = 4 + (rc * (D // 512) + fs) % 3
            tmm = None
            for c in range(nD):
                tmm = mm(PS[bank][:, 0:512],
                         Bb[:, R * c + 128 * rc:R * c + 128 * rc + 128],
                         A[:, D * c + 512 * fs:D * c + 512 * fs + 512],
                         start=(c == 0), stop=(c == nD - 1),
                         waits=[tevs[min(c // 4, len(tevs) - 1)], t_wq_done] +
                               ([ps_busy[bank]] if c == 0 else []),
                         inc=("T" if c == nD - 1 else None))
            ybof = D + 512 * (fs % 2)
            ty1 = act(MbB[:, ybof:ybof + 512], PS[bank][:, 0:512],
                      scale=MS[:, MS_xsws:MS_xsws + 1],
                      waits=[tmm, tcb, ybf_busy[fs % 2]])
            ps_busy[bank] = ty1
            ty2 = vop(lambda e, fs=fs, ybof=ybof:
                      e.tensor_tensor(MbB[:, ybof:ybof + 512],
                                      MbB[:, ybof:ybof + 512],
                                      MbB[:, 512 * fs:512 * fs + 512], op=ALU.add),
                      waits=[ty1, t_fb_done])
            ty3 = vop(lambda e, fs=fs, ybof=ybof, CX=CX, ZR=ZR:
                      e.tensor_tensor(ZR[:, 512 * fs:512 * fs + 512],
                                      CX[:, 512 * fs:512 * fs + 512],
                                      MbB[:, ybof:ybof + 512], op=ALU.add),
                      waits=[ty2, tq1, zr_busy[pair],
                             t_gmax if rc < 2 else None])
            ybf_busy[fs % 2] = ty3
            t_z.append(ty3)
        tbn = []
        for i in range(D // 512):
            tbn.append(vop(lambda e, i=i, ZR=ZR:
                           e.bn_stats(MS[:, MS_bn + 6 * i:MS_bn + 6 * i + 6],
                                      ZR[:, 512 * i:512 * i + 512]),
                           waits=[t_z[i]]))
        tag = vop(lambda e: e.bn_aggr(MS[:, MS_ag:MS_ag + 2],
                                      MS[:, MS_bn:MS_bn + 6 * (D // 512)]),
                  waits=tbn)
        tnm = vop(lambda e: e.tensor_scalar(MS[:, MS_nm:MS_nm + 1],
                                            MS[:, MS_ag:MS_ag + 1], -1.0, None,
                                            op0=ALU.mult), waits=[tag])
        tsq = act(MS[:, MS_rs:MS_rs + 1], MS[:, MS_ag + 1:MS_ag + 2], AF.Sqrt,
                  bias=MS[:, MS_eps:MS_eps + 1], waits=[tag])
        trsx = vop(lambda e: e.reciprocal(MS[:, MS_rs:MS_rs + 1],
                                          MS[:, MS_rs:MS_rs + 1]), waits=[tsq])
        tln = vop(lambda e, ZR=ZR: e.tensor_scalar(ZR[:, :], ZR[:, :],
                                                   MS[:, MS_nm:MS_nm + 1],
                                                   MS[:, MS_rs:MS_rs + 1],
                                                   op0=ALU.add, op1=ALU.mult),
                  waits=[tnm, trsx])
        tdo = p.dma("gpsimd", out_hbm[128 * rc:128 * rc + 128, :], ZR[:, :],
                    waits=[tln], chan="out%d" % pair)
        out_busy = tdo
        zr_busy[pair] = tdo
        pair_busy[pair] = t_z[-1]

    p.wait_only("gpsimd", [out_busy, p.now("d_out0"), p.now("d_out1")])
    p.finalize()
    es.close()
    return nc


def stage_inputs(cfg, x_full, attn_w, attn_b, proj_w, proj_b, gate, w0, w1,
                 ffn_w, ffn_b):
    import ml_dtypes
    D, R = cfg["D"], cfg["R"]
    nFS = cfg["nFS"]
    id_f32 = np.eye(128, dtype=np.float32)
    id_bf = np.eye(128, dtype=ml_dtypes.bfloat16)
    attn_b_t = np.ascontiguousarray(attn_b.reshape(nFS, 128).T)
    shared = {
        "attn_w": np.ascontiguousarray(attn_w),
        "attn_b_t": attn_b_t,
        "b_v_row": np.ascontiguousarray(attn_b[2 * D:3 * D][None, :]),
        "proj_w": np.ascontiguousarray(proj_w),
        "b_p_row": np.ascontiguousarray(proj_b[None, :]),
        "gate_row": np.ascontiguousarray(gate[None, :]),
        "w01": np.ascontiguousarray(
            np.concatenate([np.asarray(w0).reshape(1), np.asarray(w1).reshape(1)])[None, :]),
        "ffn_wT": np.ascontiguousarray(ffn_w.T),
        "ffn_b_row": np.ascontiguousarray(ffn_b[None, :]),
        "id_f32": id_f32,
        "id_bf": id_bf,
    }
    maps = []
    for c in range(cfg["n_cores"]):
        b, h = c // 2, c % 2
        xs = x_full[b, h * R:(h + 1) * R, :]
        m = dict(shared)
        m["x_row"] = np.ascontiguousarray(xs)
        m["xT"] = np.ascontiguousarray(xs.T)
        maps.append(m)
    return maps


def kernel(L0_attn_hidden_states, L0_attn_c_attn_w, L0_attn_c_attn_b,
           L0_attn_c_proj_w, L0_attn_c_proj_b, L0_attn_cam_gate,
           L0_attn_cam_w0, L0_attn_cam_w1, L0_ffn_weight, L0_ffn_bias):
    from concourse.bass_utils import run_bass_kernel_spmd
    cfg = make_cfg(D=2048, S=4096, n_cores=8)
    x = np.asarray(L0_attn_hidden_states)
    maps = stage_inputs(cfg, x,
                        np.asarray(L0_attn_c_attn_w), np.asarray(L0_attn_c_attn_b),
                        np.asarray(L0_attn_c_proj_w), np.asarray(L0_attn_c_proj_b),
                        np.asarray(L0_attn_cam_gate), np.asarray(L0_attn_cam_w0),
                        np.asarray(L0_attn_cam_w1), np.asarray(L0_ffn_weight),
                        np.asarray(L0_ffn_bias))
    nc = build(cfg)
    res = run_bass_kernel_spmd(nc, maps, core_ids=list(range(8)))
    R = cfg["R"]
    out = np.empty((4, 4096, 2048), dtype=np.float32)
    for c in range(8):
        b, h = c // 2, c % 2
        out[b, h * R:(h + 1) * R, :] = res.results[c]["out"]
    return out
